# version 16
# baseline (speedup 1.0000x reference)
import os
from contextlib import ExitStack

import numpy as np
import ml_dtypes

import concourse.bass as bass
import concourse.mybir as mybir
from concourse.bass_utils import run_bass_kernel_spmd

F32 = mybir.dt.float32
BF16 = mybir.dt.bfloat16
AF = mybir.ActivationFunctionType
ALU = mybir.AluOpType
AX = mybir.AxisListType

N_SUB = int(os.environ.get("KMK_NSUB", "4"))
DBG = int(os.environ.get("KMK_DBG", "0"))
N_CORES = 8
S = 2048
D = 2048
NH = 16
DH = 128
NCH = D // 128
SCALE = 1.0 / float(np.sqrt(DH))
EPS = 1e-5

ENGS = ("sync", "tensor", "vector", "scalar", "gpsimd")
PARANOID = int(os.environ.get("KMK_PARANOID", "0"))

_UID = [0]


def uname(base):
    _UID[0] += 1
    return f"{base}_{_UID[0]}"


class KB:

    ROLL = 30000
    DROLL = 2800

    NDSEM = 8

    def __init__(self, nc):
        self.nc = nc
        self.streams = {e: [] for e in ENGS}
        self.sem = {e: nc.alloc_semaphore(uname(f"s_{e}")) for e in ENGS}
        self.cnt = {e: 0 for e in ENGS}
        self.dsem = {q: [nc.alloc_semaphore(uname(f"d_{q}{i}")) for i in range(self.NDSEM)]
                     for q in ("sync", "gpsimd")}
        self.dcnt = {q: [0] * self.NDSEM for q in ("sync", "gpsimd")}
        self.dix = {"sync": 0, "gpsimd": 0}
        self.waited = {e: {} for e in ENGS}

    def _emit_waits(self, e, waits):
        todo = []
        wmap = self.waited[e]
        for ev in waits or ():
            if ev is None:
                continue
            sem, val = ev
            k = sem.name
            if wmap.get(k, -1) < val:
                wmap[k] = val
                todo.append((sem, val))
        return todo

    def op(self, e, fn, waits=None, inc=True):
        if PARANOID in (1, 3):
            waits = list(waits or []) + self.join()
        if self.cnt[e] >= self.ROLL:
            self.sem[e] = self.nc.alloc_semaphore(uname(f"s_{e}"))
            self.cnt[e] = 0
        todo = self._emit_waits(e, waits)
        if inc:
            self.cnt[e] += 1
        tgt = self.cnt[e]
        sem = self.sem[e]

        def thunk(eng, todo=todo, fn=fn, inc=inc, sem=sem):
            for (s_, v_) in todo:
                eng.wait_ge(s_, v_)
            ret = fn(eng)
            if inc:
                last = ret[-1] if isinstance(ret, tuple) else ret
                last.then_inc(sem, 1)

        self.streams[e].append(thunk)
        return (sem, tgt) if inc else None

    def dma(self, q, out_ap, in_ap, waits=None):
        if PARANOID in (1, 2):
            waits = list(waits or []) + self.join()
        elif PARANOID == 4:
            waits = list(waits or []) + [(self.sem[e], self.cnt[e])
                                         for e in ENGS if self.cnt[e] > 0]
        j = self.dix[q] % self.NDSEM
        self.dix[q] += 1
        if self.dcnt[q][j] >= self.DROLL:
            self.dsem[q][j] = self.nc.alloc_semaphore(uname(f"d_{q}{j}"))
            self.dcnt[q][j] = 0
        todo = self._emit_waits(q, waits)
        self.dcnt[q][j] += 1
        tgt = 16 * self.dcnt[q][j]
        dsem = self.dsem[q][j]

        def thunk(eng, todo=todo, out_ap=out_ap, in_ap=in_ap, dsem=dsem):
            for (s_, v_) in todo:
                eng.wait_ge(s_, v_)
            eng.dma_start(out_ap, in_ap).then_inc(dsem, 16)

        self.streams[q].append(thunk)
        return (dsem, tgt)

    def join(self):
        evs = [(self.sem[e], self.cnt[e]) for e in ENGS if self.cnt[e] > 0]
        for q in self.dcnt:
            evs += [(self.dsem[q][j], 16 * self.dcnt[q][j])
                    for j in range(self.NDSEM) if self.dcnt[q][j] > 0]
        return evs

    def build(self):
        with self.nc.Block() as block:
            for name in ENGS:
                thunks = self.streams[name]

                def body(e, thunks=thunks):
                    for f in thunks:
                        f(e)

                getattr(block, name)(body)


class Ring:
    def __init__(self, bufs, entry=None):
        self.bufs = list(bufs)
        self.n = len(self.bufs)
        self.ev = {}
        self.entry = list(entry) if entry else []

    def buf(self, i):
        return self.bufs[i % self.n]

    def guard(self, i):
        if i < self.n:
            return list(self.entry)
        g = self.ev.get(i - self.n)
        return list(g) if g else []

    def done(self, i, *evs):
        self.ev[i] = [e for e in evs if e is not None]


def _ap2(t, off, d1, d2):
    return bass.AP(t, off, [list(d1), list(d2)])


def _ap3(t, off, d1, d2, d3):
    return bass.AP(t, off, [list(d1), list(d2), list(d3)])


def build_program():
    nc = bass.Bass("TRN2", target_bir_lowering=False)

    d_x = nc.dram_tensor("x", [2, D, S], F32, kind="ExternalInput")
    d_y = nc.dram_tensor("y", [2, D, S], F32, kind="ExternalOutput")

    w_in = {}
    for nm in ("aq", "ak", "av", "ao", "bq", "bk", "bv", "bo"):
        w_in[nm] = nc.dram_tensor(f"w_{nm}", [16, 128, 2048], F32, kind="ExternalInput")
    w_in["g"] = nc.dram_tensor("w_g", [16, 16, 128, 512], F32, kind="ExternalInput")
    w_in["u"] = nc.dram_tensor("w_u", [16, 16, 128, 512], F32, kind="ExternalInput")
    w_in["d"] = nc.dram_tensor("w_d", [16, 128, 64, 128], F32, kind="ExternalInput")

    d_idb = nc.dram_tensor("ident_bf", [128, 128], BF16, kind="ExternalInput")
    d_idf = nc.dram_tensor("ident_f32", [128, 128], F32, kind="ExternalInput")
    d_ones = nc.dram_tensor("ones_col", [128, 1], F32, kind="ExternalInput")
    d_onesb = nc.dram_tensor("ones_bf", [128, 1], BF16, kind="ExternalInput")
    d_srow = nc.dram_tensor("scale_row", [1, 128], F32, kind="ExternalInput")
    d_orow = nc.dram_tensor("ones_row", [1, 128], F32, kind="ExternalInput")
    d_nsrow = nc.dram_tensor("nscale_row", [1, 128], F32, kind="ExternalInput")
    d_mask = nc.dram_tensor("mask01", [4, 128, 512], BF16, kind="ExternalInput")

    if DBG:
        d_dbg_att = nc.dram_tensor("dbg_att", [128, 16], F32, kind="ExternalOutput")
        d_dbg_nrm = nc.dram_tensor("dbg_nrm", [1, 80], F32, kind="ExternalOutput")
        d_dbg_l = nc.dram_tensor("dbg_l", [1, 512], F32, kind="ExternalOutput")
        d_dbg_pt = nc.dram_tensor("dbg_pt", [128, 512], BF16, kind="ExternalOutput")
        d_dbg_row = nc.dram_tensor("dbg_row", [4, 1, 768], F32, kind="ExternalOutput")
        d_dbg_xn = nc.dram_tensor("dbg_xn", [128, 16, 1024], BF16, kind="ExternalOutput")
        d_dbg_hst = nc.dram_tensor("dbg_hst", [128, 64, 1024], BF16, kind="ExternalOutput")
        d_dbg_sg = nc.dram_tensor("dbg_sg", [2, 128, 512], F32, kind="ExternalOutput")
        d_dbg_h = nc.dram_tensor("dbg_h", [2, 128, 512], BF16, kind="ExternalOutput")
        d_dbg_hs = nc.dram_tensor("dbg_hs", [2, 128, 512], BF16, kind="ExternalOutput")
    d_R = nc.dram_tensor("R", [2, D, S], F32, kind="Internal")
    d_XN = nc.dram_tensor("XN", [D, S], BF16, kind="Internal")
    d_KS = nc.dram_tensor("KS", [NH, DH, S], BF16, kind="Internal")
    d_VS = nc.dram_tensor("VS", [S // 128, 128, D], BF16, kind="Internal")
    w_bf = {}
    for nm in ("aq", "ak", "av", "ao", "bq", "bk", "bv", "bo"):
        w_bf[nm] = nc.dram_tensor(f"wb_{nm}", [16, 128, 2048], BF16, kind="Internal")
    w_bf["g"] = nc.dram_tensor("wb_g", [16, 16, 128, 512], BF16, kind="Internal")
    w_bf["u"] = nc.dram_tensor("wb_u", [16, 16, 128, 512], BF16, kind="Internal")
    w_bf["d"] = nc.dram_tensor("wb_d", [16, 128, 64, 128], BF16, kind="Internal")

    kb = KB(nc)
    alloc = ExitStack()

    def sb(name, shape, dtype):
        return alloc.enter_context(nc.sbuf_tensor(uname(name), shape, dtype))

    c_idb = sb("c_idb", [128, 128], BF16)
    c_ones = sb("c_ones", [128, 1], F32)
    c_onesb = sb("c_onesb", [128, 1], BF16)
    c_srow = sb("c_srow", [1, 128], F32)
    c_nsrow = sb("c_nsrow", [1, 128], F32)
    c_orow = sb("c_orow", [1, 128], F32)

    kb.dma("sync", c_idb[:, :], d_idb.ap()[:, :])
    kb.dma("sync", c_ones[:, :], d_ones.ap()[:, :])
    kb.dma("sync", c_onesb[:, :], d_onesb.ap()[:, :])
    kb.dma("sync", c_srow[:, :], d_srow.ap()[:, :])
    kb.dma("sync", c_nsrow[:, :], d_nsrow.ap()[:, :])
    ev_c = kb.dma("sync", c_orow[:, :], d_orow.ap()[:, :])
    EV_CONST = [ev_c]

    ev_precast = {}
    stg = alloc.enter_context(
        nc.sbuf_tensor(uname("stg"), [128, 4, 1024], BF16, side="right"))
    stg_ring = Ring([0, 1, 2, 3])
    stg_it = [0]
    pc_queue = []

    def precast_enqueue(nm):
        srcT = w_in[nm]
        if nm in ("g", "u"):
            chunks = [(ic * 16 * 128 * 512 + q * 2 * 128 * 512,
                       [[512, 128], [128 * 512, 2], [1, 512]], 1024)
                      for ic in range(16) for q in range(8)]
        elif nm == "d":
            chunks = [(ds * 128 * 64 * 128 + q * 8 * 128,
                       [[64 * 128, 128], [128, 8], [1, 128]], 1024)
                      for ds in range(16) for q in range(8)]
        else:
            chunks = [(h * 128 * 2048 + half * 1024,
                       [[2048, 128], [1, 1024]], 1024)
                      for h in range(16) for half in range(2)]
        for ci, (off, dims, width) in enumerate(chunks):
            pc_queue.append((nm, off, dims, width, ci == len(chunks) - 1))
        _ = srcT

    def precast_pump(k):
        n = 0
        while pc_queue and n < k:
            nm, off, dims, width, is_last = pc_queue.pop(0)
            i = stg_it[0]
            stg_it[0] += 1
            sl = stg_ring.buf(i)
            g = stg_ring.guard(i)
            evl = kb.dma("gpsimd", stg[:, sl, 0:width],
                         bass.AP(w_in[nm], off, [list(d) for d in dims]), waits=g)
            evs = kb.dma("sync", bass.AP(w_bf[nm], off, [list(d) for d in dims]),
                         stg[:, sl, 0:width], waits=[evl])
            stg_ring.done(i, evs)
            if is_last:
                ev_precast[nm] = evs
            n += 1

    def precast_drain(nm):
        while nm not in ev_precast:
            precast_pump(8)

    for nm in ("ak", "av", "aq", "ao", "g", "u", "d", "bq", "bk", "bv", "bo"):
        precast_enqueue(nm)
    precast_drain("ak")
    precast_drain("av")

    def emit_ln_full(W, res_ap_of_hc, res_ev, xn_write, pstat, pbc, sqbuf, rowbuf,
                     accb, entry=None, acc_guard=None, stat_guard=None):
        ev_sq = [None] * NCH
        ev_ac = [None] * NCH
        for hc in range(NCH):
            w = list(res_ev)
            if hc >= 2:
                w.append(ev_ac[hc - 2])
            elif entry:
                w += entry
            ev_sq[hc] = kb.op("scalar",
                              lambda e, hc=hc: e.activation(sqbuf[:, hc % 2, :],
                                                            res_ap_of_hc(hc), AF.Square),
                              waits=w)
            wa = [ev_sq[hc]] + list(res_ev)
            if hc == 0:
                if acc_guard:
                    wa += acc_guard
                ev_ac[hc] = kb.op("vector", lambda e: (
                    e.tensor_copy(accb[:, 0, :], res_ap_of_hc(0)),
                    e.tensor_copy(accb[:, 1, :], sqbuf[:, 0, :]),
                ), waits=wa)
            else:
                ev_ac[hc] = kb.op("vector", lambda e, hc=hc: (
                    e.tensor_tensor(accb[:, 0, :], accb[:, 0, :], res_ap_of_hc(hc),
                                    op=ALU.add),
                    e.tensor_tensor(accb[:, 1, :], accb[:, 1, :], sqbuf[:, hc % 2, :],
                                    op=ALU.add),
                ), waits=wa)
        w_st = [ev_ac[-1]] + EV_CONST
        if stat_guard:
            w_st += stat_guard
        ev_stat = kb.op("tensor", lambda e: e.matmul(
            pstat[0:1, 0:2 * W], c_ones[:, :],
            _ap2(accb, 0, [2 * W, 128], [1, 2 * W]),
            start=True, stop=True), waits=w_st)
        ev = kb.op("vector", lambda e: (
            e.tensor_scalar(rowbuf[0:1, 0:W], pstat[0:1, 0:W], 1.0 / D, None,
                            op0=ALU.mult),
            e.tensor_scalar(rowbuf[0:1, W:2 * W], pstat[0:1, W:2 * W], 1.0 / D, EPS,
                            op0=ALU.mult, op1=ALU.add),
        ), waits=[ev_stat])
        ev_statrd = ev
        ev = kb.op("vector", lambda e: e.tensor_tensor(
            rowbuf[0:1, 2 * W:3 * W], rowbuf[0:1, 0:W], rowbuf[0:1, 0:W], op=ALU.mult),
            waits=[ev])
        ev = kb.op("vector", lambda e: e.tensor_tensor(
            rowbuf[0:1, W:2 * W], rowbuf[0:1, W:2 * W], rowbuf[0:1, 2 * W:3 * W],
            op=ALU.subtract), waits=[ev])
        ev = kb.op("scalar", lambda e: e.activation(
            rowbuf[0:1, 2 * W:3 * W], rowbuf[0:1, W:2 * W], AF.Sqrt),
            waits=[ev])
        ev = kb.op("vector", lambda e: e.reciprocal(
            rowbuf[0:1, W:2 * W], rowbuf[0:1, 2 * W:3 * W]), waits=[ev])
        ev_bc = kb.op("tensor", lambda e: e.matmul(
            pbc[:, 0:2 * W], c_orow[0:1, :], rowbuf[0:1, 0:2 * W],
            start=True, stop=True), waits=[ev] + EV_CONST)
        xn_evs = []
        ev_prev = [None, None]
        for hc in range(NCH):
            w = [ev_bc] + list(res_ev)
            if ev_prev[hc % 2]:
                w.append(ev_prev[hc % 2])
            evt = kb.op("vector", lambda e, hc=hc: e.tensor_tensor(
                sqbuf[:, hc % 2, :], res_ap_of_hc(hc), pbc[:, 0:W], op=ALU.subtract),
                waits=w)
            evx = kb.op("vector", lambda e, hc=hc: e.tensor_tensor(
                xn_write(hc), sqbuf[:, hc % 2, :], pbc[:, W:2 * W], op=ALU.mult),
                waits=[evt])
            ev_prev[hc % 2] = evx
            xn_evs.append(evx)
        return xn_evs, ev_stat, ev_statrd

    def emit_attn(wq, wk, wv, wo, src_R, dst, causal, prev_join, w_ready):
        WT = 256
        attn_ctx = ExitStack()
        c_mask = None
        if causal:
            c_mask = attn_ctx.enter_context(
                nc.sbuf_tensor(uname("c_mask"), [128, 4, 512], BF16))
            ev_ml = kb.dma("sync",
                           _ap3(c_mask, 0, [4 * 512, 128], [512, 4], [1, 512]),
                           _ap3(d_mask, 0, [512, 128], [128 * 512, 4], [1, 512]),
                           waits=prev_join)
            prev_join = prev_join + [ev_ml]
        for s in range(2):
            with ExitStack() as ph:
                def asb(name, shape, dtype):
                    return ph.enter_context(nc.sbuf_tensor(uname(name), shape, dtype))
                wk_sb = asb("wk_sb", [128, NH, 2048], BF16)
                wv_sb = asb("wv_sb", [128, NCH, 2048], BF16)
                res_sb = Ring([asb("resA", [128, NCH, WT], F32) for _ in range(2)],
                              entry=prev_join)
                xn_sb = asb("xn_sb", [128, NCH, WT], BF16)
                sqb = asb("sqA", [128, 2, WT], F32)
                rowb = asb("rowA", [1, 3 * WT], F32)
                accA = asb("accA", [128, 2, WT], F32)
                kst = Ring([asb("kst", [128, NH, WT], BF16) for _ in range(1)],
                           entry=prev_join)
                vst = Ring([asb("vst", [128, 2048], BF16) for _ in range(1)],
                           entry=prev_join)
                pstat = ph.enter_context(nc.psum_tensor(uname("pstat"), [1, 2 * WT], F32))
                pbc = ph.enter_context(nc.psum_tensor(uname("pbc"), [128, 2 * WT], F32))
                pk = Ring([ph.enter_context(nc.psum_tensor(uname("pk"), [128, WT], F32))
                           for _ in range(2)], entry=prev_join)
                pv = Ring([ph.enter_context(nc.psum_tensor(uname("pv"), [128, 512], F32))
                           for _ in range(2)], entry=prev_join)

                base_w = [w_ready["k"], w_ready["v"]] + prev_join
                ev_wk = ev_wv = None
                for half in range(2):
                    ev_wk = kb.dma(
                        "sync",
                        _ap3(wk_sb, half * 8 * 2048, [NH * 2048, 128], [2048, 8], [1, 2048]),
                        _ap3(wk, half * 8 * 128 * 2048, [2048, 128], [128 * 2048, 8], [1, 2048]),
                        waits=base_w)
                    ev_wv = kb.dma(
                        "sync",
                        _ap3(wv_sb, half * 8 * 2048, [NCH * 2048, 128], [2048, 8], [1, 2048]),
                        _ap3(wv, half * 8 * 128 * 2048, [2048, 128], [128 * 2048, 8], [1, 2048]),
                        waits=base_w)

                ki, vi = [0], [0]
                entry = None
                acc_g = None
                stat_g = None
                for tt in range(S // WT):
                    t0 = tt * WT
                    rbuf = res_sb.buf(tt)
                    ev_res = kb.dma(
                        "sync",
                        _ap3(rbuf, 0, [NCH * WT, 128], [WT, NCH], [1, WT]),
                        _ap3(src_R, s * D * S + t0, [S, 128], [128 * S, NCH], [1, WT]),
                        waits=res_sb.guard(tt) + prev_join)
                    xn_evs, ev_stat, ev_statrd = emit_ln_full(
                        WT, lambda hc, rbuf=rbuf: rbuf[:, hc, :], [ev_res],
                        lambda hc: xn_sb[:, hc, :], pstat, pbc, sqb, rowb,
                        accA, entry=entry, acc_guard=acc_g, stat_guard=stat_g)
                    acc_g = [ev_stat]
                    stat_g = [ev_statrd]
                    ev_xn_st = kb.dma(
                        "gpsimd",
                        _ap3(d_XN, t0, [S, 128], [128 * S, NCH], [1, WT]),
                        _ap3(xn_sb, 0, [NCH * WT, 128], [WT, NCH], [1, WT]),
                        waits=xn_evs)
                    kbuf = kst.buf(tt)
                    kg = kst.guard(tt)
                    ev_k_ev = []
                    for h in range(NH):
                        i = ki[0]
                        ki[0] += 1
                        pkb = pk.buf(i)
                        evm = kb.op("tensor", lambda e, h=h, pkb=pkb: tuple(
                            e.matmul(pkb[:, :], wk_sb[:, h, hc * 128:(hc + 1) * 128],
                                     xn_sb[:, hc, :], start=(hc == 0), stop=(hc == NCH - 1))
                            for hc in range(NCH)),
                            waits=[ev_wk] + xn_evs + pk.guard(i))
                        eve = kb.op("scalar", lambda e, h=h, pkb=pkb, kbuf=kbuf:
                                    e.copy(kbuf[:, h, :], pkb[:, :]), waits=[evm] + kg)
                        pk.done(i, eve)
                        ev_k_ev.append(eve)
                    ev_k_st = kb.dma(
                        "gpsimd",
                        _ap3(d_KS, t0, [S, 128], [128 * S, NH], [1, WT]),
                        _ap3(kbuf, 0, [NH * WT, 128], [WT, NH], [1, WT]),
                        waits=ev_k_ev)
                    kst.done(tt, ev_k_st)
                    for tsub in range(WT // 128):
                        vbuf = vst.buf(vi[0])
                        vg = vst.guard(vi[0])
                        ev_v_ev = []
                        for dc in range(4):
                            i = vi[0] * 4 + dc
                            pvb = pv.buf(i)
                            evm = kb.op("tensor", lambda e, tsub=tsub, dc=dc, pvb=pvb: tuple(
                                e.matmul(pvb[:, :], xn_sb[:, hc, tsub * 128:(tsub + 1) * 128],
                                         wv_sb[:, hc, dc * 512:(dc + 1) * 512],
                                         start=(hc == 0), stop=(hc == NCH - 1))
                                for hc in range(NCH)),
                                waits=[ev_wv] + xn_evs + pv.guard(i))
                            eve = kb.op("scalar", lambda e, dc=dc, pvb=pvb, vbuf=vbuf:
                                        e.copy(vbuf[:, dc * 512:(dc + 1) * 512], pvb[:, :]),
                                        waits=[evm] + vg)
                            pv.done(i, eve)
                            ev_v_ev.append(eve)
                        ksub = t0 // 128 + tsub
                        ev_v_st = kb.dma(
                            "gpsimd",
                            _ap2(d_VS, ksub * 128 * D, [D, 128], [1, D]),
                            _ap2(vbuf, 0, [2048, 128], [1, 2048]),
                            waits=ev_v_ev)
                        vst.done(vi[0], ev_v_st)
                        vi[0] += 1
                    res_sb.done(tt, xn_evs[-1])
                    entry = [ev_xn_st]
                    precast_pump(6)

            joinA = kb.join()

            with ExitStack() as ph:
                def asb(name, shape, dtype):
                    return ph.enter_context(nc.sbuf_tensor(uname(name), shape, dtype))
                K_sb = asb("K_sb", [128, NH, S], BF16)
                V_sb = asb("V_sb", [128, S // 128, D], BF16)
                attnT = asb("attnT", [128, NCH, 512], BF16)
                Q_all = asb("Q_all", [128, NH, 512], BF16)
                xn_q = attnT
                wqo_sb = Ring([asb("wqoB", [128, 2048], BF16) for _ in range(3)],
                              entry=joinA)
                pt_sb = Ring([asb("ptl", [128, 512], BF16) for _ in range(5)],
                             entry=joinA)
                rb_sb = asb("rb_sb", [128, 512], F32)
                nr_sb = asb("nr_sb", [1, 512], F32)
                nr_sb2 = asb("nr_sb2", [1, 512], F32)
                accL = asb("accL", [128, 2, 512], F32)
                qq_sb = asb("qq_sb", [128, 2, 512], BF16)
                norm_sb = asb("norm_sb", [1, 80], F32)
                mh_sb = asb("mh_sb", [128, 16], F32)
                res2 = Ring([asb("res2", [128, 512], F32) for _ in range(2)],
                            entry=joinA)

                p_t = Ring([ph.enter_context(nc.psum_tensor(uname("pT"), [128, 512], F32))
                            for _ in range(4)], entry=joinA)
                p_pv = ph.enter_context(nc.psum_tensor(uname("ppv"), [128, 512], F32))
                p_q = ph.enter_context(nc.psum_tensor(uname("pq"), [128, 512], F32))
                p_s6 = ph.enter_context(nc.psum_tensor(uname("ps6"), [1, 512], F32))
                p_bc = ph.enter_context(nc.psum_tensor(uname("pbc2"), [128, 512], F32))

                ev_Kq = []
                ev_Vq = []

                def emit_kv_loads():
                    for qtr in range(4):
                        ev_Kq.append(kb.dma(
                            "sync" if qtr % 2 == 0 else "gpsimd",
                            _ap3(K_sb, qtr * 4 * S, [NH * S, 128], [S, 4], [1, S]),
                            _ap3(d_KS, qtr * 4 * 128 * S, [S, 128], [128 * S, 4], [1, S]),
                            waits=joinA))
                        ev_Vq.append(kb.dma(
                            "gpsimd" if qtr % 2 == 0 else "sync",
                            _ap3(V_sb, qtr * 4 * D, [(S // 128) * D, 128], [D, 4], [1, D]),
                            _ap3(d_VS, qtr * 4 * 128 * D, [D, 128], [128 * D, 4], [1, D]),
                            waits=joinA))

                ev_k2 = []
                ev_sq_prev = None
                ev_qq_prev = None

                def emit_knorms():
                    nonlocal ev_sq_prev, ev_qq_prev
                    for h in range(NH):
                        ev_rc_ = None
                        for c in range(4):
                            w_sq = [ev_Kq[h // 4]] + joinA
                            if ev_qq_prev:
                                w_sq.append(ev_qq_prev)
                            ev_s = kb.op("scalar", lambda e, h=h, c=c: e.activation(
                                qq_sb[:, 0, :], K_sb[:, h, c * 512:(c + 1) * 512], AF.Square),
                                waits=w_sq)
                            ev_mm = kb.op("tensor", lambda e, c=c: e.matmul(
                                p_s6[0:1, 0:512], c_onesb[:, :], qq_sb[:, 0, :],
                                start=True, stop=True),
                                waits=[ev_s] + ([ev_rc_] if ev_rc_ else []) + joinA)
                            ev_qq_prev = ev_mm
                            ev_rc_ = kb.op("vector", lambda e, c=c: e.reduce_max(
                                norm_sb[0:1, 64 + c:65 + c], p_s6[0:1, 0:512], axis=AX.X),
                                waits=[ev_mm])
                        ev_r = kb.op("vector", lambda e, h=h: e.reduce_max(
                            norm_sb[0:1, h:h + 1], norm_sb[0:1, 64:68], axis=AX.X),
                            waits=[ev_rc_])
                        ev_k2.append(ev_r)
                        ev_qq_prev = ev_r
                wi, pi, sti, r2i = [0], [0], [0], [0]
                p1i = [0]
                ev_lc_prev = None
                ev_rbc_prev = None
                pv_ring = [p_pv, p_q]
                pv_guard = [None, None]
                attnT_free = [None]
                lm_guard = [None, None]
                qn_guard = [None, None]
                fin_pend = []
                fin2_pend = []
                nr2_sb = [nr_sb, nr_sb2]

                def fin_f1(item):
                    nonlocal ev_lc_prev
                    h, ev_a, ev_pv_last, p_v = item
                    w_lm = [ev_a, ev_mhc]
                    if ev_lc_prev:
                        w_lm.append(ev_lc_prev)
                    ev_lm = kb.op("tensor", lambda e, h=h: e.matmul(
                        p_s6[0:1, 0:512], c_ones[:, :], accL[:, h % 2, :],
                        start=True, stop=True), waits=w_lm)
                    lm_guard[h % 2] = ev_lm
                    ev_lc = kb.op("vector", lambda e, h=h: e.tensor_copy(
                        nr2_sb[h % 2][0:1, :], p_s6[0:1, 0:512]),
                        waits=[ev_lm])
                    ev_lc_prev = ev_lc
                    ev_rc = kb.op("vector", lambda e, h=h: e.reciprocal(
                        nr2_sb[h % 2][0:1, :], nr2_sb[h % 2][0:1, :]), waits=[ev_lc])
                    fin2_pend.append((h, ev_rc, ev_pv_last, p_v))

                def fin_f2(item):
                    nonlocal ev_rbc_prev
                    h, ev_rc, ev_pv_last, p_v = item
                    ev_rbm = kb.op("tensor", lambda e, h=h: e.matmul(
                        p_bc[:, :], c_orow[0:1, :], nr2_sb[h % 2][0:1, :],
                        start=True, stop=True),
                        waits=[ev_rc] + ([ev_rbc_prev] if ev_rbc_prev else []))
                    ev_rbc = kb.op("vector", lambda e: e.tensor_copy(
                        rb_sb[:, :], p_bc[:, :]), waits=[ev_rbm])
                    ev_rbc_prev = ev_rbc
                    w_at = [ev_pv_last, ev_rbc]
                    if attnT_free[0]:
                        w_at.append(attnT_free[0])
                    ev_at = kb.op("vector", lambda e, h=h, p_v=p_v: e.tensor_tensor(
                        attnT[:, h, :], p_v[:, :], rb_sb[:, :], op=ALU.mult),
                        waits=w_at)
                    pv_guard[h % 2] = ev_at
                    head_done[h] = ev_at
                for Q in range(4):
                    q0 = Q * 512
                    ev_xnq = kb.dma(
                        "sync",
                        _ap3(xn_q, 0, [NCH * 512, 128], [512, NCH], [1, 512]),
                        _ap3(d_XN, q0, [S, 128], [128 * S, NCH], [1, 512]),
                        waits=joinA + ([attnT_free[0]] if attnT_free[0] else []))
                    if Q == 0:
                        emit_kv_loads()
                    head_done = []
                    ev_q2 = []
                    qn_pend = []

                    def qn_flush(item):
                        nonlocal ev_qq_prev
                        h2, ev_s2 = item
                        w_mm = [ev_s2]
                        if ev_qq_prev:
                            w_mm.append(ev_qq_prev)
                        if ev_lc_prev:
                            w_mm.append(ev_lc_prev)
                        ev_mm = kb.op("tensor", lambda e, h2=h2: e.matmul(
                            p_s6[0:1, 0:512], c_onesb[:, :], qq_sb[:, h2 % 2, :],
                            start=True, stop=True), waits=w_mm)
                        qn_guard[h2 % 2] = ev_mm
                        ev_r = kb.op("vector", lambda e, h2=h2: e.reduce_max(
                            norm_sb[0:1, 16 + h2:17 + h2], p_s6[0:1, 0:512], axis=AX.X),
                            waits=[ev_mm])
                        ev_qq_prev = ev_r
                        ev_q2.append(ev_r)

                    for h in range(NH):
                        i = wi[0]
                        wi[0] += 1
                        wbuf = wqo_sb.buf(i)
                        ev_wq = kb.dma("sync" if h % 2 == 0 else "gpsimd", wbuf[:, :],
                                       _ap2(wq, h * 128 * 2048, [2048, 128], [1, 2048]),
                                       waits=wqo_sb.guard(i) + [w_ready["q"]])
                        pqb = pv_ring[h % 2]
                        evm = kb.op("tensor", lambda e, wbuf=wbuf, pqb=pqb: tuple(
                            e.matmul(pqb[:, :], wbuf[:, hc * 128:(hc + 1) * 128],
                                     xn_q[:, hc, :], start=(hc == 0), stop=(hc == NCH - 1))
                            for hc in range(NCH)),
                            waits=[ev_wq, ev_xnq]
                            + ([pv_guard[h % 2]] if pv_guard[h % 2] else []))
                        eve = kb.op("vector", lambda e, h=h, pqb=pqb:
                                    e.tensor_copy(Q_all[:, h, :], pqb[:, :]), waits=[evm])
                        pv_guard[h % 2] = eve
                        wqo_sb.done(i, evm)
                        head_done.append(eve)
                        w_qq = [eve]
                        if qn_guard[h % 2]:
                            w_qq.append(qn_guard[h % 2])
                        ev_s = kb.op("vector", lambda e, h=h: e.tensor_tensor(
                            qq_sb[:, h % 2, :], Q_all[:, h, :], Q_all[:, h, :],
                            op=ALU.mult), waits=w_qq)
                        qn_pend.append((h, ev_s))
                        if len(qn_pend) > 1:
                            qn_flush(qn_pend.pop(0))
                    for it_ in qn_pend:
                        qn_flush(it_)
                    qn_pend.clear()
                    if Q == 0:
                        emit_knorms()
                    ev_m2 = kb.op("vector", lambda e: e.tensor_tensor(
                        norm_sb[0:1, 32:48], norm_sb[0:1, 16:32], norm_sb[0:1, 0:16],
                        op=ALU.mult), waits=ev_q2 + ev_k2)
                    ev_m = kb.op("scalar", lambda e: e.activation(
                        norm_sb[0:1, 48:64], norm_sb[0:1, 32:48], AF.Sqrt),
                        waits=[ev_m2])
                    ev_mhm = kb.op("tensor", lambda e: e.matmul(
                        p_bc[:, 0:16], c_nsrow[0:1, :], norm_sb[0:1, 48:64],
                        start=True, stop=True),
                        waits=[ev_m] + ([ev_rbc_prev] if ev_rbc_prev else []))
                    ev_mhc = kb.op("vector", lambda e: e.tensor_copy(
                        mh_sb[:, :], p_bc[:, 0:16]), waits=[ev_mhm])
                    if DBG and s == 0 and Q == 0 and causal:
                        kb.dma("gpsimd", d_dbg_att.ap()[:, :], mh_sb[:, :], waits=[ev_mhc])
                        kb.dma("gpsimd", d_dbg_nrm.ap()[:, :], norm_sb[0:1, 0:80], waits=[ev_mhc])

                    for h in range(NH):
                        while fin2_pend and fin2_pend[0][0] <= h - 2:
                            fin_f2(fin2_pend.pop(0))
                        ev_qh = head_done[h]
                        n_ks = 4 * (Q + 1) if causal else 16
                        p_v = pv_ring[h % 2]
                        LA = 3
                        ev_a = None
                        ev_pv_last = None
                        st1 = {}
                        for kx in range(n_ks + LA):
                            if kx < n_ks:
                                ksx = kx
                                i = sti[0]
                                sti[0] += 1
                                pb = p_t.buf(i)
                                evm = kb.op("tensor", lambda e, h=h, ksx=ksx, pb=pb: e.matmul(
                                    pb[:, :], K_sb[:, h, ksx * 128:(ksx + 1) * 128],
                                    Q_all[:, h, :], start=True, stop=True),
                                    waits=[ev_qh, ev_Kq[h // 4]] + p_t.guard(i))
                                pti = pi[0]
                                pi[0] += 1
                                ptb = pt_sb.buf(pti)
                                eve = kb.op("scalar", lambda e, pb=pb, ptb=ptb, h=h: e.activation(
                                    ptb[:, :], pb[:, :], AF.Exp,
                                    bias=mh_sb[:, h:h + 1], scale=SCALE),
                                    waits=[evm, ev_mhc] + pt_sb.guard(pti))
                                p_t.done(i, eve)
                                ev_pt = eve
                                if causal and ksx >= 4 * Q:
                                    r = ksx - 4 * Q
                                    ev_pt = kb.op("vector", lambda e, r=r, ptb=ptb: e.tensor_tensor(
                                        ptb[:, :], ptb[:, :], c_mask[:, r, :], op=ALU.mult),
                                        waits=[eve] + EV_CONST)
                                st1[kx] = (pti, ptb, ev_pt)
                            if kx >= LA:
                                ksx = kx - LA
                                first = (ksx == 0)
                                last = (ksx == n_ks - 1)
                                pti, ptb, ev_pt = st1.pop(ksx)
                                w_a = [ev_pt]
                                if first and lm_guard[h % 2]:
                                    w_a.append(lm_guard[h % 2])
                                if first:
                                    ev_a = kb.op("vector", lambda e, h=h, ptb=ptb:
                                                 e.tensor_copy(accL[:, h % 2, :], ptb[:, :]),
                                                 waits=w_a)
                                else:
                                    ev_a = kb.op("vector", lambda e, h=h, ptb=ptb:
                                                 e.tensor_tensor(accL[:, h % 2, :],
                                                                 accL[:, h % 2, :],
                                                                 ptb[:, :], op=ALU.add),
                                                 waits=w_a)
                                w_pv = [ev_pt, ev_Vq[ksx // 4]]
                                if first and pv_guard[h % 2]:
                                    w_pv.append(pv_guard[h % 2])
                                ev_pv_last = kb.op("tensor",
                                                   lambda e, ksx=ksx, h=h, ptb=ptb, first=first, last=last, p_v=p_v:
                                                   e.matmul(p_v[:, :],
                                                            V_sb[:, ksx, h * 128:(h + 1) * 128],
                                                            ptb[:, :], start=first, stop=last),
                                                   waits=w_pv)
                                pt_sb.done(pti, ev_a, ev_pv_last)
                        fin_pend.append((h, ev_a, ev_pv_last, p_v))
                        if len(fin_pend) > 1:
                            fin_f1(fin_pend.pop(0))
                        precast_pump(3)
                    for it_ in fin_pend:
                        fin_f1(it_)
                    fin_pend.clear()
                    for it_ in fin2_pend:
                        fin_f2(it_)
                    fin2_pend.clear()
                    ev_o_last = None
                    for osub in range(NCH):
                        i = wi[0]
                        wi[0] += 1
                        wbuf = wqo_sb.buf(i)
                        ev_wo = kb.dma("sync", wbuf[:, :],
                                       _ap2(wo, osub * 128 * 2048, [2048, 128], [1, 2048]),
                                       waits=wqo_sb.guard(i) + [w_ready["o"]])
                        ri = r2i[0]
                        r2i[0] += 1
                        rbuf = res2.buf(ri)
                        ev_rl = kb.dma("sync", rbuf[:, :],
                                       _ap2(src_R, s * D * S + osub * 128 * S + q0,
                                            [S, 128], [1, 512]),
                                       waits=res2.guard(ri))
                        pb = pv_ring[osub % 2]
                        w_om = [ev_wo] + head_done
                        if pv_guard[osub % 2]:
                            w_om.append(pv_guard[osub % 2])
                        evm = kb.op("tensor", lambda e, wbuf=wbuf, pb=pb: tuple(
                            e.matmul(pb[:, :], wbuf[:, hc * 128:(hc + 1) * 128],
                                     attnT[:, hc, :], start=(hc == 0), stop=(hc == NCH - 1))
                            for hc in range(NCH)),
                            waits=w_om)
                        wqo_sb.done(i, evm)
                        ev_o_last = evm
                        ev_add = kb.op("vector", lambda e, pb=pb, rbuf=rbuf: e.tensor_tensor(
                            rbuf[:, :], rbuf[:, :], pb[:, :], op=ALU.add),
                            waits=[evm, ev_rl])
                        pv_guard[osub % 2] = ev_add
                        ev_st = kb.dma("gpsimd",
                                       _ap2(dst, s * D * S + osub * 128 * S + q0,
                                            [S, 128], [1, 512]),
                                       rbuf[:, :], waits=[ev_add])
                        res2.done(ri, ev_st)
                    attnT_free[0] = ev_o_last
            prev_join = kb.join()
        attn_ctx.close()
        return prev_join

    def emit_ffn(wg, wu, wd, src_R, dst, prev_join, w_ready):
        TB = 512
        for s in range(2):
            for tb in range(S // TB):
                tb0 = tb * TB
                with ExitStack() as phh:
                    hsT = phh.enter_context(nc.sbuf_tensor(uname("hsT"), [128, 64, TB], BF16))
                    with ExitStack() as ph:
                        xn_tb = ph.enter_context(
                            nc.sbuf_tensor(uname("xn_tb"), [128, NCH, TB], BF16))
                        with ExitStack() as p1s:
                            res_sb = Ring([p1s.enter_context(
                                nc.sbuf_tensor(uname("fres"), [128, NCH, 256], F32))
                                for _ in range(2)], entry=prev_join)
                            sqb = p1s.enter_context(
                                nc.sbuf_tensor(uname("fsq"), [128, 2, 256], F32))
                            rowb = p1s.enter_context(
                                nc.sbuf_tensor(uname("frow"), [1, 768], F32))
                            accF = p1s.enter_context(
                                nc.sbuf_tensor(uname("facc"), [128, 2, 256], F32))
                            pstat = p1s.enter_context(
                                nc.psum_tensor(uname("fstat"), [1, 512], F32))
                            pbc = p1s.enter_context(
                                nc.psum_tensor(uname("fbc"), [128, 512], F32))
                            entry = None
                            acc_g = None
                            stat_g = None
                            for tq in range(TB // 256):
                                t0 = tb0 + tq * 256
                                rbuf = res_sb.buf(tq)
                                ev_res = kb.dma(
                                    "sync",
                                    _ap3(rbuf, 0, [NCH * 256, 128], [256, NCH], [1, 256]),
                                    _ap3(src_R, s * D * S + t0, [S, 128], [128 * S, NCH], [1, 256]),
                                    waits=res_sb.guard(tq) + prev_join)
                                xn_evs, ev_stat, ev_statrd = emit_ln_full(
                                    256, lambda hc, rbuf=rbuf: rbuf[:, hc, :], [ev_res],
                                    lambda hc, tq=tq: xn_tb[:, hc, tq * 256:(tq + 1) * 256],
                                    pstat, pbc, sqb, rowb, accF, entry=entry,
                                    acc_guard=acc_g, stat_guard=stat_g)
                                acc_g = [ev_stat]
                                stat_g = [ev_statrd]
                                res_sb.done(tq, xn_evs[-1])
                                entry = [xn_evs[-1]]
                            ev_xn = kb.join()
                        with ExitStack() as p2s:
                            def asb2(name, shape, dtype):
                                return p2s.enter_context(
                                    nc.sbuf_tensor(uname(name), shape, dtype))
                            wgb = Ring([asb2("wgb", [128, NCH, 512], BF16) for _ in range(2)],
                                       entry=ev_xn)
                            wub = Ring([asb2("wub", [128, NCH, 512], BF16) for _ in range(2)],
                                       entry=ev_xn)
                            sg_sb = Ring([asb2("sg", [128, 512], F32) for _ in range(2)],
                                         entry=ev_xn)
                            h_sb = Ring([asb2("hsb", [128, 512], BF16) for _ in range(2)],
                                        entry=ev_xn)
                            a_sb = asb2("a_sb", [128, 512], BF16)
                            t_sb = asb2("t_sb", [128, 4, 128], BF16)
                            hs_sb = Ring([asb2("hss", [128, 512], BF16) for _ in range(2)],
                                         entry=ev_xn)
                            pg = Ring([p2s.enter_context(
                                nc.psum_tensor(uname("pg"), [128, 512], F32)) for _ in range(2)],
                                entry=ev_xn)
                            pu = Ring([p2s.enter_context(
                                nc.psum_tensor(uname("pu"), [128, 512], F32)) for _ in range(2)],
                                entry=ev_xn)
                            ptp = Ring([p2s.enter_context(
                                nc.psum_tensor(uname("ptp"), [128, 512], BF16)) for _ in range(2)],
                                entry=ev_xn)

                            ti = [0]
                            ev2_prev = None
                            pend = []

                            def flush_tr(item):
                                i, ic, ts, hsb, ev3 = item
                                ptb = ptp.buf(i)
                                evt = kb.op("tensor", lambda e, hsb=hsb, ptb=ptb: tuple(
                                    e.transpose(ptb[:, j * 128:(j + 1) * 128],
                                                hsb[:, j * 128:(j + 1) * 128], c_idb[:, :])
                                    for j in range(4)),
                                    waits=[ev3] + EV_CONST + ptp.guard(i))
                                hs_sb.done(i, evt)
                                eve = kb.op("vector", lambda e, ic=ic, ts=ts, ptb=ptb:
                                            e.tensor_copy(
                                                _ap3(hsT, (ic * 4) * TB + ts * 128,
                                                     [64 * TB, 128], [TB, 4], [1, 128]),
                                                _ap3(ptb, 0, [512, 128], [128, 4], [1, 128])),
                                            waits=[evt])
                                ptp.done(i, eve)

                            for ic in range(16):
                                precast_pump(6)
                                ev_w = []
                                for (ring, dram, wkey) in ((wgb, wg, "g"), (wub, wu, "u")):
                                    bufw = ring.buf(ic)
                                    evw = kb.dma(
                                        "sync",
                                        _ap3(bufw, 0, [NCH * 512, 128], [512, NCH], [1, 512]),
                                        _ap3(dram, ic * NCH * 128 * 512,
                                             [512, 128], [128 * 512, NCH], [1, 512]),
                                        waits=ring.guard(ic) + [w_ready[wkey]])
                                    ev_w.append(evw)
                                for ts in range(TB // 128):
                                    i = ti[0]
                                    ti[0] += 1
                                    pgb, pub = pg.buf(i), pu.buf(i)
                                    evg = kb.op("tensor", lambda e, ic=ic, ts=ts, pgb=pgb: tuple(
                                        e.matmul(pgb[:, :],
                                                 xn_tb[:, hc, ts * 128:(ts + 1) * 128],
                                                 wgb.buf(ic)[:, hc, :],
                                                 start=(hc == 0), stop=(hc == NCH - 1))
                                        for hc in range(NCH)),
                                        waits=ev_w + ev_xn + pg.guard(i))
                                    evu = kb.op("tensor", lambda e, ic=ic, ts=ts, pub=pub: tuple(
                                        e.matmul(pub[:, :],
                                                 xn_tb[:, hc, ts * 128:(ts + 1) * 128],
                                                 wub.buf(ic)[:, hc, :],
                                                 start=(hc == 0), stop=(hc == NCH - 1))
                                        for hc in range(NCH)),
                                        waits=ev_w + ev_xn + pu.guard(i))
                                    if ts == TB // 128 - 1:
                                        wgb.done(ic, evg)
                                        wub.done(ic, evu)
                                    sgb = sg_sb.buf(i)
                                    evs_ = kb.op("scalar", lambda e, pgb=pgb, sgb=sgb:
                                                 e.activation(sgb[:, :], pgb[:, :], AF.Silu),
                                                 waits=[evg] + sg_sb.guard(i))
                                    pg.done(i, evs_)
                                    hb = h_sb.buf(i)
                                    evh = kb.op("vector", lambda e, pub=pub, sgb=sgb, hb=hb:
                                                e.tensor_tensor(hb[:, :], sgb[:, :], pub[:, :],
                                                                op=ALU.mult),
                                                waits=[evs_, evu] + h_sb.guard(i))
                                    pu.done(i, evh)
                                    sg_sb.done(i, evh)
                                    w_abs = [evh]
                                    if ev2_prev:
                                        w_abs.append(ev2_prev)
                                    eva = kb.op("scalar", lambda e, hb=hb: e.activation(
                                        a_sb[:, :], hb[:, :], AF.Abs), waits=w_abs)
                                    ev1 = kb.op("vector", lambda e: (
                                        e.tensor_tensor(t_sb[:, 0, :], a_sb[:, 0:128],
                                                        a_sb[:, 128:256], op=ALU.max),
                                        e.tensor_tensor(t_sb[:, 1, :], a_sb[:, 0:128],
                                                        a_sb[:, 128:256], op=ALU.min),
                                        e.tensor_tensor(t_sb[:, 2, :], a_sb[:, 256:384],
                                                        a_sb[:, 384:512], op=ALU.max),
                                        e.tensor_tensor(t_sb[:, 3, :], a_sb[:, 256:384],
                                                        a_sb[:, 384:512], op=ALU.min),
                                        e.tensor_tensor(t_sb[:, 0, :], t_sb[:, 0, :],
                                                        t_sb[:, 2, :], op=ALU.min),
                                        e.tensor_tensor(t_sb[:, 1, :], t_sb[:, 1, :],
                                                        t_sb[:, 3, :], op=ALU.max),
                                        e.tensor_tensor(t_sb[:, 0, :], t_sb[:, 0, :],
                                                        t_sb[:, 1, :], op=ALU.max),
                                    ), waits=[eva])
                                    hsb = hs_sb.buf(i)
                                    ev2 = kb.op("vector", lambda e, hsb=hsb: tuple(
                                        e.tensor_tensor(hsb[:, j * 128:(j + 1) * 128],
                                                        a_sb[:, j * 128:(j + 1) * 128],
                                                        t_sb[:, 0, :], op=ALU.is_ge)
                                        for j in range(4)), waits=[ev1] + hs_sb.guard(i))
                                    ev2_prev = ev2
                                    ev3 = kb.op("vector", lambda e, hb=hb, hsb=hsb:
                                                e.tensor_tensor(hsb[:, :], hsb[:, :], hb[:, :],
                                                                op=ALU.mult),
                                                waits=[ev2])
                                    h_sb.done(i, ev3)
                                    pend.append((i, ic, ts, hsb, ev3))
                                    if len(pend) > 1:
                                        flush_tr(pend.pop(0))
                            for p_ in pend:
                                flush_tr(p_)
                            pend.clear()
                            ev_hsT = kb.join()
                    with ExitStack() as p3s:
                        def asb3(name, shape, dtype):
                            return p3s.enter_context(nc.sbuf_tensor(uname(name), shape, dtype))
                        wdb = Ring([asb3("wdb", [128, 64, 128], BF16) for _ in range(2)],
                                   entry=ev_hsT)
                        res3 = Ring([asb3("res3", [128, 512], F32) for _ in range(4)],
                                    entry=ev_hsT)
                        pd = Ring([p3s.enter_context(
                            nc.psum_tensor(uname("pd"), [128, 512], F32)) for _ in range(4)],
                            entry=ev_hsT)
                        for dsub in range(NCH):
                            bufw = wdb.buf(dsub)
                            evw = kb.dma("sync",
                                         _ap3(bufw, 0, [64 * 128, 128], [128, 64], [1, 128]),
                                         _ap3(wd, dsub * 128 * 64 * 128,
                                              [64 * 128, 128], [128, 64], [1, 128]),
                                         waits=wdb.guard(dsub) + [w_ready["d"]])
                            pdb = pd.buf(dsub)
                            ev_mm = kb.op(
                                "tensor", lambda e, bufw=bufw, pdb=pdb: tuple(
                                    e.matmul(pdb[:, :], bufw[:, isub, :],
                                             hsT[:, isub, :],
                                             start=(isub == 0), stop=(isub == 63))
                                    for isub in range(64)),
                                waits=[evw] + ev_hsT + pd.guard(dsub))
                            wdb.done(dsub, ev_mm)
                            rbuf = res3.buf(dsub)
                            ev_rl = kb.dma(
                                "sync", rbuf[:, :],
                                _ap2(src_R, s * D * S + dsub * 128 * S + tb0,
                                     [S, 128], [1, 512]),
                                waits=res3.guard(dsub))
                            ev_add = kb.op("vector", lambda e, pdb=pdb, rbuf=rbuf:
                                           e.tensor_tensor(rbuf[:, :], rbuf[:, :], pdb[:, :],
                                                           op=ALU.add),
                                           waits=[ev_mm, ev_rl])
                            pd.done(dsub, ev_add)
                            ev_st = kb.dma(
                                "gpsimd",
                                _ap2(dst, s * D * S + dsub * 128 * S + tb0,
                                     [S, 128], [1, 512]),
                                rbuf[:, :], waits=[ev_add])
                            res3.done(dsub, ev_st)
                prev_join = kb.join()
        return prev_join

    jn = list(EV_CONST)
    layers = [
        ("attn", "aq", "ak", "av", "ao", True),
        ("ffn",),
        ("attn", "bq", "bk", "bv", "bo", False),
        ("ffn",),
    ]
    src = d_x
    for li in range(N_SUB):
        dst = d_y if li == N_SUB - 1 else d_R
        if layers[li][0] == "attn":
            _, nq, nk, nv, no, causal = layers[li]
            for t in (nq, nk, nv, no):
                precast_drain(t)
            wr = {"q": ev_precast[nq], "k": ev_precast[nk],
                  "v": ev_precast[nv], "o": ev_precast[no]}
            jn = emit_attn(w_bf[nq], w_bf[nk], w_bf[nv], w_bf[no],
                           src, dst, causal, jn, wr)
        else:
            for t in ("g", "u", "d"):
                precast_drain(t)
            wr_f = {"g": ev_precast["g"], "u": ev_precast["u"], "d": ev_precast["d"]}
            jn = emit_ffn(w_bf["g"], w_bf["u"], w_bf["d"], src, dst, jn, wr_f)
        src = d_R

    final_evs = kb.join()
    kb.op("gpsimd", lambda e: e.sem_inc(kb.sem["gpsimd"], 1),
          waits=final_evs, inc=False)
    kb.op("sync", lambda e: e.sem_inc(kb.sem["sync"], 1),
          waits=final_evs, inc=False)

    with nc.allow_low_precision("bf16 matmul operands / bf16 activation stores"):
        kb.build()
    alloc.close()
    return nc


def _lhsT_tile(W):
    return np.ascontiguousarray(
        W.reshape(16, 128, 16, 128).transpose(0, 3, 2, 1).reshape(16, 128, 2048))


def _rhsT_tile(W):
    return np.ascontiguousarray(W.T.reshape(16, 128, 2048))


def _ffn_gu_tile(W):
    Wp = W.reshape(2048, 16, 128, 4).transpose(0, 1, 3, 2).reshape(2048, 8192)
    return np.ascontiguousarray(Wp.reshape(16, 128, 16, 512).transpose(2, 0, 1, 3))


def _ffn_d_tile(W):
    Wp = W.reshape(16, 128, 4, 2048).transpose(0, 2, 1, 3).reshape(8192, 2048)
    return np.ascontiguousarray(Wp.reshape(64, 128, 16, 128).transpose(2, 1, 0, 3))


def kernel(L0_attn_a_hidden_states, L0_attn_a_w_qkv, L0_attn_a_w_o,
           L0_attn_b_Wq, L0_attn_b_Wk, L0_attn_b_Wv, L0_attn_b_Wo,
           L0_ffn_w_gate, L0_ffn_w_up, L0_ffn_w_down):
    x = np.asarray(L0_attn_a_hidden_states, dtype=np.float32)
    wqkv = np.asarray(L0_attn_a_w_qkv, dtype=np.float32)
    woa = np.asarray(L0_attn_a_w_o, dtype=np.float32)

    weights = {
        "w_aq": _lhsT_tile(wqkv[0:2048]),
        "w_ak": _lhsT_tile(wqkv[2048:4096]),
        "w_av": _rhsT_tile(wqkv[4096:6144]),
        "w_ao": _lhsT_tile(woa),
        "w_bq": _lhsT_tile(np.asarray(L0_attn_b_Wq, dtype=np.float32)),
        "w_bk": _lhsT_tile(np.asarray(L0_attn_b_Wk, dtype=np.float32)),
        "w_bv": _rhsT_tile(np.asarray(L0_attn_b_Wv, dtype=np.float32)),
        "w_bo": _lhsT_tile(np.asarray(L0_attn_b_Wo, dtype=np.float32)),
        "w_g": _ffn_gu_tile(np.asarray(L0_ffn_w_gate, dtype=np.float32)),
        "w_u": _ffn_gu_tile(np.asarray(L0_ffn_w_up, dtype=np.float32)),
        "w_d": _ffn_d_tile(np.asarray(L0_ffn_w_down, dtype=np.float32)),
    }
    consts = {
        "ident_bf": np.eye(128, dtype=ml_dtypes.bfloat16),
        "ident_f32": np.eye(128, dtype=np.float32),
        "ones_col": np.ones((128, 1), dtype=np.float32),
        "ones_bf": np.ones((128, 1), dtype=ml_dtypes.bfloat16),
        "scale_row": np.full((1, 128), SCALE, dtype=np.float32),
        "nscale_row": np.full((1, 128), -SCALE, dtype=np.float32),
        "ones_row": np.ones((1, 128), dtype=np.float32),
    }
    r_ = np.arange(4)[:, None, None]
    p_ = np.arange(128)[None, :, None]
    f_ = np.arange(512)[None, None, :]
    consts["mask01"] = (f_ >= 128 * r_ + p_).astype(ml_dtypes.bfloat16)

    nc = build_program()

    in_maps = []
    for c in range(N_CORES):
        m = {"x": np.ascontiguousarray(x[2 * c:2 * c + 2].transpose(0, 2, 1))}
        m.update(weights)
        m.update(consts)
        in_maps.append(m)

    res = run_bass_kernel_spmd(nc, in_maps, core_ids=list(range(N_CORES)))
    out = np.concatenate([r["y"].transpose(0, 2, 1) for r in res.results], axis=0)
    return np.ascontiguousarray(out)


# version 18
# speedup vs baseline: 1.0920x; 1.0920x over previous
import os
from contextlib import ExitStack

import numpy as np
import ml_dtypes

import concourse.bass as bass
import concourse.mybir as mybir
from concourse.bass_utils import run_bass_kernel_spmd

F32 = mybir.dt.float32
BF16 = mybir.dt.bfloat16
AF = mybir.ActivationFunctionType
ALU = mybir.AluOpType
AX = mybir.AxisListType

N_SUB = int(os.environ.get("KMK_NSUB", "4"))
DBG = int(os.environ.get("KMK_DBG", "0"))
N_CORES = 8
S = 2048
D = 2048
NH = 16
DH = 128
NCH = D // 128
SCALE = 1.0 / float(np.sqrt(DH))
EPS = 1e-5

ENGS = ("sync", "tensor", "vector", "scalar", "gpsimd")
PARANOID = int(os.environ.get("KMK_PARANOID", "0"))

_UID = [0]


def uname(base):
    _UID[0] += 1
    return f"{base}_{_UID[0]}"


class KB:

    ROLL = 30000
    DROLL = 2800

    NDSEM = 8

    def __init__(self, nc):
        self.nc = nc
        self.streams = {e: [] for e in ENGS}
        self.sem = {e: nc.alloc_semaphore(uname(f"s_{e}")) for e in ENGS}
        self.cnt = {e: 0 for e in ENGS}
        self.dsem = {q: [nc.alloc_semaphore(uname(f"d_{q}{i}")) for i in range(self.NDSEM)]
                     for q in ("sync", "gpsimd")}
        self.dcnt = {q: [0] * self.NDSEM for q in ("sync", "gpsimd")}
        self.dix = {"sync": 0, "gpsimd": 0}
        self.waited = {e: {} for e in ENGS}

    def _emit_waits(self, e, waits):
        todo = []
        wmap = self.waited[e]
        for ev in waits or ():
            if ev is None:
                continue
            sem, val = ev
            k = sem.name
            if wmap.get(k, -1) < val:
                wmap[k] = val
                todo.append((sem, val))
        return todo

    def op(self, e, fn, waits=None, inc=True):
        if PARANOID in (1, 3):
            waits = list(waits or []) + self.join()
        if self.cnt[e] >= self.ROLL:
            self.sem[e] = self.nc.alloc_semaphore(uname(f"s_{e}"))
            self.cnt[e] = 0
        todo = self._emit_waits(e, waits)
        if inc:
            self.cnt[e] += 1
        tgt = self.cnt[e]
        sem = self.sem[e]

        def thunk(eng, todo=todo, fn=fn, inc=inc, sem=sem):
            for (s_, v_) in todo:
                eng.wait_ge(s_, v_)
            ret = fn(eng)
            if inc:
                last = ret[-1] if isinstance(ret, tuple) else ret
                last.then_inc(sem, 1)

        self.streams[e].append(thunk)
        return (sem, tgt) if inc else None

    def dma(self, q, out_ap, in_ap, waits=None):
        if PARANOID in (1, 2):
            waits = list(waits or []) + self.join()
        elif PARANOID == 4:
            waits = list(waits or []) + [(self.sem[e], self.cnt[e])
                                         for e in ENGS if self.cnt[e] > 0]
        j = self.dix[q] % self.NDSEM
        self.dix[q] += 1
        if self.dcnt[q][j] >= self.DROLL:
            self.dsem[q][j] = self.nc.alloc_semaphore(uname(f"d_{q}{j}"))
            self.dcnt[q][j] = 0
        todo = self._emit_waits(q, waits)
        self.dcnt[q][j] += 1
        tgt = 16 * self.dcnt[q][j]
        dsem = self.dsem[q][j]

        def thunk(eng, todo=todo, out_ap=out_ap, in_ap=in_ap, dsem=dsem):
            for (s_, v_) in todo:
                eng.wait_ge(s_, v_)
            eng.dma_start(out_ap, in_ap).then_inc(dsem, 16)

        self.streams[q].append(thunk)
        return (dsem, tgt)

    def join(self):
        evs = [(self.sem[e], self.cnt[e]) for e in ENGS if self.cnt[e] > 0]
        for q in self.dcnt:
            evs += [(self.dsem[q][j], 16 * self.dcnt[q][j])
                    for j in range(self.NDSEM) if self.dcnt[q][j] > 0]
        return evs

    def build(self):
        with self.nc.Block() as block:
            for name in ENGS:
                thunks = self.streams[name]

                def body(e, thunks=thunks):
                    for f in thunks:
                        f(e)

                getattr(block, name)(body)


class Ring:
    def __init__(self, bufs, entry=None):
        self.bufs = list(bufs)
        self.n = len(self.bufs)
        self.ev = {}
        self.entry = list(entry) if entry else []

    def buf(self, i):
        return self.bufs[i % self.n]

    def guard(self, i):
        if i < self.n:
            return list(self.entry)
        g = self.ev.get(i - self.n)
        return list(g) if g else []

    def done(self, i, *evs):
        self.ev[i] = [e for e in evs if e is not None]


def _ap2(t, off, d1, d2):
    return bass.AP(t, off, [list(d1), list(d2)])


def _ap3(t, off, d1, d2, d3):
    return bass.AP(t, off, [list(d1), list(d2), list(d3)])


def build_program():
    nc = bass.Bass("TRN2", target_bir_lowering=False)

    d_x = nc.dram_tensor("x", [2, D, S], F32, kind="ExternalInput")
    d_y = nc.dram_tensor("y", [2, D, S], F32, kind="ExternalOutput")

    w_in = {}
    for nm in ("aq", "ak", "av", "ao", "bq", "bk", "bv", "bo"):
        w_in[nm] = nc.dram_tensor(f"w_{nm}", [16, 128, 2048], F32, kind="ExternalInput")
    w_in["g"] = nc.dram_tensor("w_g", [16, 16, 128, 512], F32, kind="ExternalInput")
    w_in["u"] = nc.dram_tensor("w_u", [16, 16, 128, 512], F32, kind="ExternalInput")
    w_in["d"] = nc.dram_tensor("w_d", [16, 128, 64, 128], F32, kind="ExternalInput")

    d_idb = nc.dram_tensor("ident_bf", [128, 128], BF16, kind="ExternalInput")
    d_idf = nc.dram_tensor("ident_f32", [128, 128], F32, kind="ExternalInput")
    d_ones = nc.dram_tensor("ones_col", [128, 1], F32, kind="ExternalInput")
    d_onesb = nc.dram_tensor("ones_bf", [128, 1], BF16, kind="ExternalInput")
    d_srow = nc.dram_tensor("scale_row", [1, 128], F32, kind="ExternalInput")
    d_orow = nc.dram_tensor("ones_row", [1, 128], F32, kind="ExternalInput")
    d_nsrow = nc.dram_tensor("nscale_row", [1, 128], F32, kind="ExternalInput")
    d_mask = nc.dram_tensor("mask01", [4, 128, 512], BF16, kind="ExternalInput")

    if DBG:
        d_dbg_att = nc.dram_tensor("dbg_att", [128, 16], F32, kind="ExternalOutput")
        d_dbg_nrm = nc.dram_tensor("dbg_nrm", [1, 80], F32, kind="ExternalOutput")
        d_dbg_l = nc.dram_tensor("dbg_l", [1, 512], F32, kind="ExternalOutput")
        d_dbg_pt = nc.dram_tensor("dbg_pt", [128, 512], BF16, kind="ExternalOutput")
        d_dbg_row = nc.dram_tensor("dbg_row", [4, 1, 768], F32, kind="ExternalOutput")
        d_dbg_xn = nc.dram_tensor("dbg_xn", [128, 16, 1024], BF16, kind="ExternalOutput")
        d_dbg_hst = nc.dram_tensor("dbg_hst", [128, 64, 1024], BF16, kind="ExternalOutput")
        d_dbg_sg = nc.dram_tensor("dbg_sg", [2, 128, 512], F32, kind="ExternalOutput")
        d_dbg_h = nc.dram_tensor("dbg_h", [2, 128, 512], BF16, kind="ExternalOutput")
        d_dbg_hs = nc.dram_tensor("dbg_hs", [2, 128, 512], BF16, kind="ExternalOutput")
    d_R = nc.dram_tensor("R", [2, D, S], F32, kind="Internal")
    d_XN = nc.dram_tensor("XN", [D, S], BF16, kind="Internal")
    d_KS = nc.dram_tensor("KS", [NH, DH, S], BF16, kind="Internal")
    d_VS = nc.dram_tensor("VS", [S // 128, 128, D], BF16, kind="Internal")
    w_bf = {}
    for nm in ("aq", "ak", "av", "ao", "bq", "bk", "bv", "bo"):
        w_bf[nm] = nc.dram_tensor(f"wb_{nm}", [16, 128, 2048], BF16, kind="Internal")
    w_bf["g"] = nc.dram_tensor("wb_g", [16, 16, 128, 512], BF16, kind="Internal")
    w_bf["u"] = nc.dram_tensor("wb_u", [16, 16, 128, 512], BF16, kind="Internal")
    w_bf["d"] = nc.dram_tensor("wb_d", [16, 128, 64, 128], BF16, kind="Internal")

    kb = KB(nc)
    alloc = ExitStack()

    def sb(name, shape, dtype):
        return alloc.enter_context(nc.sbuf_tensor(uname(name), shape, dtype))

    c_idb = sb("c_idb", [128, 128], BF16)
    c_ones = sb("c_ones", [128, 1], F32)
    c_onesb = sb("c_onesb", [128, 1], BF16)
    c_srow = sb("c_srow", [1, 128], F32)
    c_nsrow = sb("c_nsrow", [1, 128], F32)
    c_orow = sb("c_orow", [1, 128], F32)

    kb.dma("sync", c_idb[:, :], d_idb.ap()[:, :])
    kb.dma("sync", c_ones[:, :], d_ones.ap()[:, :])
    kb.dma("sync", c_onesb[:, :], d_onesb.ap()[:, :])
    kb.dma("sync", c_srow[:, :], d_srow.ap()[:, :])
    kb.dma("sync", c_nsrow[:, :], d_nsrow.ap()[:, :])
    ev_c = kb.dma("sync", c_orow[:, :], d_orow.ap()[:, :])
    EV_CONST = [ev_c]

    ev_precast = {}
    stg = alloc.enter_context(
        nc.sbuf_tensor(uname("stg"), [128, 4, 1024], BF16, side="right"))
    stg_ring = Ring([0, 1, 2, 3])
    stg_it = [0]
    pc_queue = []

    def precast_enqueue(nm):
        srcT = w_in[nm]
        if nm in ("g", "u"):
            chunks = [(ic * 16 * 128 * 512 + q * 2 * 128 * 512,
                       [[512, 128], [128 * 512, 2], [1, 512]], 1024)
                      for ic in range(16) for q in range(8)]
        elif nm == "d":
            chunks = [(ds * 128 * 64 * 128 + q * 8 * 128,
                       [[64 * 128, 128], [128, 8], [1, 128]], 1024)
                      for ds in range(16) for q in range(8)]
        else:
            chunks = [(h * 128 * 2048 + half * 1024,
                       [[2048, 128], [1, 1024]], 1024)
                      for h in range(16) for half in range(2)]
        for ci, (off, dims, width) in enumerate(chunks):
            pc_queue.append((nm, off, dims, width, ci == len(chunks) - 1))
        _ = srcT

    def precast_pump(k):
        n = 0
        while pc_queue and n < k:
            nm, off, dims, width, is_last = pc_queue.pop(0)
            i = stg_it[0]
            stg_it[0] += 1
            sl = stg_ring.buf(i)
            g = stg_ring.guard(i)
            evl = kb.dma("gpsimd", stg[:, sl, 0:width],
                         bass.AP(w_in[nm], off, [list(d) for d in dims]), waits=g)
            evs = kb.dma("sync", bass.AP(w_bf[nm], off, [list(d) for d in dims]),
                         stg[:, sl, 0:width], waits=[evl])
            stg_ring.done(i, evs)
            if is_last:
                ev_precast[nm] = evs
            n += 1

    def precast_drain(nm):
        while nm not in ev_precast:
            precast_pump(8)

    for nm in ("ak", "av", "aq", "ao", "g", "u", "d", "bq", "bk", "bv", "bo"):
        precast_enqueue(nm)
    precast_drain("ak")
    precast_drain("av")

    def emit_ln_full(W, res_ap_of_hc, res_ev, xn_write, pstat, pbc, sqbuf, rowbuf,
                     accb, entry=None, acc_guard=None, stat_guard=None):
        ev_sq = [None] * NCH
        ev_ac = [None] * NCH
        for hc in range(NCH):
            w = list(res_ev)
            if hc >= 2:
                w.append(ev_ac[hc - 2])
            elif entry:
                w += entry
            ev_sq[hc] = kb.op("scalar",
                              lambda e, hc=hc: e.activation(sqbuf[:, hc % 2, :],
                                                            res_ap_of_hc(hc), AF.Square),
                              waits=w)
            wa = [ev_sq[hc]] + list(res_ev)
            if hc == 0:
                if acc_guard:
                    wa += acc_guard
                ev_ac[hc] = kb.op("vector", lambda e: (
                    e.tensor_copy(accb[:, 0, :], res_ap_of_hc(0)),
                    e.tensor_copy(accb[:, 1, :], sqbuf[:, 0, :]),
                ), waits=wa)
            else:
                ev_ac[hc] = kb.op("vector", lambda e, hc=hc: (
                    e.tensor_tensor(accb[:, 0, :], accb[:, 0, :], res_ap_of_hc(hc),
                                    op=ALU.add),
                    e.tensor_tensor(accb[:, 1, :], accb[:, 1, :], sqbuf[:, hc % 2, :],
                                    op=ALU.add),
                ), waits=wa)
        w_st = [ev_ac[-1]] + EV_CONST
        if stat_guard:
            w_st += stat_guard
        ev_stat = kb.op("tensor", lambda e: e.matmul(
            pstat[0:1, 0:2 * W], c_ones[:, :],
            _ap2(accb, 0, [2 * W, 128], [1, 2 * W]),
            start=True, stop=True), waits=w_st)
        ev = kb.op("vector", lambda e: (
            e.tensor_scalar(rowbuf[0:1, 0:W], pstat[0:1, 0:W], 1.0 / D, None,
                            op0=ALU.mult),
            e.tensor_scalar(rowbuf[0:1, W:2 * W], pstat[0:1, W:2 * W], 1.0 / D, EPS,
                            op0=ALU.mult, op1=ALU.add),
        ), waits=[ev_stat])
        ev_statrd = ev
        ev = kb.op("vector", lambda e: e.tensor_tensor(
            rowbuf[0:1, 2 * W:3 * W], rowbuf[0:1, 0:W], rowbuf[0:1, 0:W], op=ALU.mult),
            waits=[ev])
        ev = kb.op("vector", lambda e: e.tensor_tensor(
            rowbuf[0:1, W:2 * W], rowbuf[0:1, W:2 * W], rowbuf[0:1, 2 * W:3 * W],
            op=ALU.subtract), waits=[ev])
        ev = kb.op("scalar", lambda e: e.activation(
            rowbuf[0:1, 2 * W:3 * W], rowbuf[0:1, W:2 * W], AF.Sqrt),
            waits=[ev])
        ev = kb.op("vector", lambda e: e.reciprocal(
            rowbuf[0:1, W:2 * W], rowbuf[0:1, 2 * W:3 * W]), waits=[ev])
        ev_bc = kb.op("tensor", lambda e: e.matmul(
            pbc[:, 0:2 * W], c_orow[0:1, :], rowbuf[0:1, 0:2 * W],
            start=True, stop=True), waits=[ev] + EV_CONST)
        xn_evs = []
        ev_prev = [None, None]
        for hc in range(NCH):
            w = [ev_bc] + list(res_ev)
            if ev_prev[hc % 2]:
                w.append(ev_prev[hc % 2])
            evt = kb.op("vector", lambda e, hc=hc: e.tensor_tensor(
                sqbuf[:, hc % 2, :], res_ap_of_hc(hc), pbc[:, 0:W], op=ALU.subtract),
                waits=w)
            evx = kb.op("vector", lambda e, hc=hc: e.tensor_tensor(
                xn_write(hc), sqbuf[:, hc % 2, :], pbc[:, W:2 * W], op=ALU.mult),
                waits=[evt])
            ev_prev[hc % 2] = evx
            xn_evs.append(evx)
        return xn_evs, ev_stat, ev_statrd

    def emit_attn(wq, wk, wv, wo, src_R, dst, causal, prev_join, w_ready):
        WT = 256
        attn_ctx = ExitStack()
        c_mask = None
        if causal:
            c_mask = attn_ctx.enter_context(
                nc.sbuf_tensor(uname("c_mask"), [128, 4, 512], BF16))
            ev_ml = kb.dma("sync",
                           _ap3(c_mask, 0, [4 * 512, 128], [512, 4], [1, 512]),
                           _ap3(d_mask, 0, [512, 128], [128 * 512, 4], [1, 512]),
                           waits=prev_join)
            prev_join = prev_join + [ev_ml]
        for s in range(2):
            with ExitStack() as ph:
                def asb(name, shape, dtype):
                    return ph.enter_context(nc.sbuf_tensor(uname(name), shape, dtype))
                wk_sb = asb("wk_sb", [128, NH, 2048], BF16)
                wv_sb = asb("wv_sb", [128, NCH, 2048], BF16)
                res_sb = Ring([asb("resA", [128, NCH, WT], F32) for _ in range(2)],
                              entry=prev_join)
                xn_sb = asb("xn_sb", [128, NCH, WT], BF16)
                sqb = asb("sqA", [128, 2, WT], F32)
                rowb = asb("rowA", [1, 3 * WT], F32)
                accA = asb("accA", [128, 2, WT], F32)
                kst = Ring([asb("kst", [128, NH, WT], BF16) for _ in range(1)],
                           entry=prev_join)
                vst = Ring([asb("vst", [128, 2048], BF16) for _ in range(1)],
                           entry=prev_join)
                pstat = ph.enter_context(nc.psum_tensor(uname("pstat"), [1, 2 * WT], F32))
                pbc = ph.enter_context(nc.psum_tensor(uname("pbc"), [128, 2 * WT], F32))
                pk = Ring([ph.enter_context(nc.psum_tensor(uname("pk"), [128, WT], F32))
                           for _ in range(2)], entry=prev_join)
                pv = Ring([ph.enter_context(nc.psum_tensor(uname("pv"), [128, 512], F32))
                           for _ in range(2)], entry=prev_join)

                base_w = [w_ready["k"], w_ready["v"]] + prev_join
                ev_wk = ev_wv = None
                for half in range(2):
                    ev_wk = kb.dma(
                        "sync",
                        _ap3(wk_sb, half * 8 * 2048, [NH * 2048, 128], [2048, 8], [1, 2048]),
                        _ap3(wk, half * 8 * 128 * 2048, [2048, 128], [128 * 2048, 8], [1, 2048]),
                        waits=base_w)
                    ev_wv = kb.dma(
                        "sync",
                        _ap3(wv_sb, half * 8 * 2048, [NCH * 2048, 128], [2048, 8], [1, 2048]),
                        _ap3(wv, half * 8 * 128 * 2048, [2048, 128], [128 * 2048, 8], [1, 2048]),
                        waits=base_w)

                ki, vi = [0], [0]
                entry = None
                acc_g = None
                stat_g = None
                for tt in range(S // WT):
                    t0 = tt * WT
                    rbuf = res_sb.buf(tt)
                    ev_res = kb.dma(
                        "sync",
                        _ap3(rbuf, 0, [NCH * WT, 128], [WT, NCH], [1, WT]),
                        _ap3(src_R, s * D * S + t0, [S, 128], [128 * S, NCH], [1, WT]),
                        waits=res_sb.guard(tt) + prev_join)
                    xn_evs, ev_stat, ev_statrd = emit_ln_full(
                        WT, lambda hc, rbuf=rbuf: rbuf[:, hc, :], [ev_res],
                        lambda hc: xn_sb[:, hc, :], pstat, pbc, sqb, rowb,
                        accA, entry=entry, acc_guard=acc_g, stat_guard=stat_g)
                    acc_g = [ev_stat]
                    stat_g = [ev_statrd]
                    ev_xn_st = kb.dma(
                        "gpsimd",
                        _ap3(d_XN, t0, [S, 128], [128 * S, NCH], [1, WT]),
                        _ap3(xn_sb, 0, [NCH * WT, 128], [WT, NCH], [1, WT]),
                        waits=xn_evs)
                    kbuf = kst.buf(tt)
                    kg = kst.guard(tt)
                    ev_k_ev = []
                    for h in range(NH):
                        i = ki[0]
                        ki[0] += 1
                        pkb = pk.buf(i)
                        evm = kb.op("tensor", lambda e, h=h, pkb=pkb: tuple(
                            e.matmul(pkb[:, :], wk_sb[:, h, hc * 128:(hc + 1) * 128],
                                     xn_sb[:, hc, :], start=(hc == 0), stop=(hc == NCH - 1))
                            for hc in range(NCH)),
                            waits=[ev_wk] + xn_evs + pk.guard(i))
                        eve = kb.op("scalar", lambda e, h=h, pkb=pkb, kbuf=kbuf:
                                    e.copy(kbuf[:, h, :], pkb[:, :]), waits=[evm] + kg)
                        pk.done(i, eve)
                        ev_k_ev.append(eve)
                    ev_k_st = kb.dma(
                        "gpsimd",
                        _ap3(d_KS, t0, [S, 128], [128 * S, NH], [1, WT]),
                        _ap3(kbuf, 0, [NH * WT, 128], [WT, NH], [1, WT]),
                        waits=ev_k_ev)
                    kst.done(tt, ev_k_st)
                    for tsub in range(WT // 128):
                        vbuf = vst.buf(vi[0])
                        vg = vst.guard(vi[0])
                        ev_v_ev = []
                        for dc in range(4):
                            i = vi[0] * 4 + dc
                            pvb = pv.buf(i)
                            evm = kb.op("tensor", lambda e, tsub=tsub, dc=dc, pvb=pvb: tuple(
                                e.matmul(pvb[:, :], xn_sb[:, hc, tsub * 128:(tsub + 1) * 128],
                                         wv_sb[:, hc, dc * 512:(dc + 1) * 512],
                                         start=(hc == 0), stop=(hc == NCH - 1))
                                for hc in range(NCH)),
                                waits=[ev_wv] + xn_evs + pv.guard(i))
                            eve = kb.op("scalar", lambda e, dc=dc, pvb=pvb, vbuf=vbuf:
                                        e.copy(vbuf[:, dc * 512:(dc + 1) * 512], pvb[:, :]),
                                        waits=[evm] + vg)
                            pv.done(i, eve)
                            ev_v_ev.append(eve)
                        ksub = t0 // 128 + tsub
                        ev_v_st = kb.dma(
                            "gpsimd",
                            _ap2(d_VS, ksub * 128 * D, [D, 128], [1, D]),
                            _ap2(vbuf, 0, [2048, 128], [1, 2048]),
                            waits=ev_v_ev)
                        vst.done(vi[0], ev_v_st)
                        vi[0] += 1
                    res_sb.done(tt, xn_evs[-1])
                    entry = [ev_xn_st]
                    precast_pump(6)

            joinA = kb.join()

            with ExitStack() as ph:
                def asb(name, shape, dtype):
                    return ph.enter_context(nc.sbuf_tensor(uname(name), shape, dtype))
                K_sb = asb("K_sb", [128, NH, S], BF16)
                V_sb = asb("V_sb", [128, S // 128, D], BF16)
                attnT = asb("attnT", [128, NCH, 512], BF16)
                Q_all = asb("Q_all", [128, NH, 512], BF16)
                xn_q = attnT
                wqo_sb = Ring([asb("wqoB", [128, 2048], BF16) for _ in range(3)],
                              entry=joinA)
                pt_sb = Ring([asb("ptl", [128, 512], BF16) for _ in range(5)],
                             entry=joinA)
                rb_sb = asb("rb_sb", [128, 512], F32)
                nr_sb = asb("nr_sb", [1, 512], F32)
                nr_sb2 = asb("nr_sb2", [1, 512], F32)
                accL = asb("accL", [128, 2, 512], F32)
                qq_sb = asb("qq_sb", [128, 2, 512], BF16)
                norm_sb = asb("norm_sb", [1, 80], F32)
                mh_sb = asb("mh_sb", [128, 16], F32)
                res2 = Ring([asb("res2", [128, 512], F32) for _ in range(2)],
                            entry=joinA)

                p_t = Ring([ph.enter_context(nc.psum_tensor(uname("pT"), [128, 512], F32))
                            for _ in range(4)], entry=joinA)
                p_pv = ph.enter_context(nc.psum_tensor(uname("ppv"), [128, 512], F32))
                p_q = ph.enter_context(nc.psum_tensor(uname("pq"), [128, 512], F32))
                p_s6 = ph.enter_context(nc.psum_tensor(uname("ps6"), [1, 512], F32))
                p_bc = ph.enter_context(nc.psum_tensor(uname("pbc2"), [128, 512], F32))

                ev_Kq = []
                ev_Vq = []

                def emit_k_loads():
                    for qtr in range(4):
                        ev_Kq.append(kb.dma(
                            "gpsimd",
                            _ap3(K_sb, qtr * 4 * S, [NH * S, 128], [S, 4], [1, S]),
                            _ap3(d_KS, qtr * 4 * 128 * S, [S, 128], [128 * S, 4], [1, S]),
                            waits=joinA))

                def emit_v_loads():
                    for qtr in range(4):
                        ev_Vq.append(kb.dma(
                            "sync",
                            _ap3(V_sb, qtr * 4 * D, [(S // 128) * D, 128], [D, 4], [1, D]),
                            _ap3(d_VS, qtr * 4 * 128 * D, [D, 128], [128 * D, 4], [1, D]),
                            waits=joinA))

                ev_k2 = []
                ev_sq_prev = None
                ev_qq_prev = None

                def emit_knorms():
                    nonlocal ev_sq_prev, ev_qq_prev
                    for h in range(NH):
                        ev_rc_ = None
                        for c in range(4):
                            w_sq = [ev_Kq[h // 4]] + joinA
                            if ev_qq_prev:
                                w_sq.append(ev_qq_prev)
                            ev_s = kb.op("scalar", lambda e, h=h, c=c: e.activation(
                                qq_sb[:, 0, :], K_sb[:, h, c * 512:(c + 1) * 512], AF.Square),
                                waits=w_sq)
                            ev_mm = kb.op("tensor", lambda e, c=c: e.matmul(
                                p_s6[0:1, 0:512], c_onesb[:, :], qq_sb[:, 0, :],
                                start=True, stop=True),
                                waits=[ev_s] + ([ev_rc_] if ev_rc_ else []) + joinA)
                            ev_qq_prev = ev_mm
                            ev_rc_ = kb.op("vector", lambda e, c=c: e.reduce_max(
                                norm_sb[0:1, 64 + c:65 + c], p_s6[0:1, 0:512], axis=AX.X),
                                waits=[ev_mm])
                        ev_r = kb.op("vector", lambda e, h=h: e.reduce_max(
                            norm_sb[0:1, h:h + 1], norm_sb[0:1, 64:68], axis=AX.X),
                            waits=[ev_rc_])
                        ev_k2.append(ev_r)
                        ev_qq_prev = ev_r
                wi, pi, sti, r2i = [0], [0], [0], [0]
                p1i = [0]
                ev_lc_prev = None
                ev_rbc_prev = None
                pv_ring = [p_pv, p_q]
                pv_guard = [None, None]
                attnT_free = [None]
                lm_guard = [None, None]
                qn_guard = [None, None]
                fin_pend = []
                nr2_sb = [nr_sb, nr_sb2]

                def fin_flush(item):
                    nonlocal ev_rbc_prev, ev_lc_prev
                    h, ev_a, ev_pv_last, p_v = item
                    w_lm = [ev_a, ev_mhc]
                    if ev_lc_prev:
                        w_lm.append(ev_lc_prev)
                    ev_lm = kb.op("tensor", lambda e, h=h: e.matmul(
                        p_s6[0:1, 0:512], c_ones[:, :], accL[:, h % 2, :],
                        start=True, stop=True), waits=w_lm)
                    lm_guard[h % 2] = ev_lm
                    ev_lc = kb.op("vector", lambda e, h=h: e.tensor_copy(
                        nr2_sb[h % 2][0:1, :], p_s6[0:1, 0:512]),
                        waits=[ev_lm])
                    ev_lc_prev = ev_lc
                    ev_rc = kb.op("vector", lambda e, h=h: e.reciprocal(
                        nr2_sb[h % 2][0:1, :], nr2_sb[h % 2][0:1, :]), waits=[ev_lc])
                    ev_rbm = kb.op("tensor", lambda e, h=h: e.matmul(
                        p_bc[:, :], c_orow[0:1, :], nr2_sb[h % 2][0:1, :],
                        start=True, stop=True),
                        waits=[ev_rc] + ([ev_rbc_prev] if ev_rbc_prev else []))
                    ev_rbc = kb.op("vector", lambda e: e.tensor_copy(
                        rb_sb[:, :], p_bc[:, :]), waits=[ev_rbm])
                    ev_rbc_prev = ev_rbc
                    w_at = [ev_pv_last, ev_rbc]
                    if attnT_free[0]:
                        w_at.append(attnT_free[0])
                    ev_at = kb.op("vector", lambda e, h=h, p_v=p_v: e.tensor_tensor(
                        attnT[:, h, :], p_v[:, :], rb_sb[:, :], op=ALU.mult),
                        waits=w_at)
                    pv_guard[h % 2] = ev_at
                    head_done[h] = ev_at
                for Q in range(4):
                    q0 = Q * 512
                    ev_xnq = kb.dma(
                        "sync",
                        _ap3(xn_q, 0, [NCH * 512, 128], [512, NCH], [1, 512]),
                        _ap3(d_XN, q0, [S, 128], [128 * S, NCH], [1, 512]),
                        waits=joinA + ([attnT_free[0]] if attnT_free[0] else []))
                    if Q == 0:
                        emit_k_loads()
                    head_done = []
                    ev_q2 = []
                    qn_pend = []

                    def qn_flush(item):
                        nonlocal ev_qq_prev
                        h2, ev_s2 = item
                        w_mm = [ev_s2]
                        if ev_qq_prev:
                            w_mm.append(ev_qq_prev)
                        if ev_lc_prev:
                            w_mm.append(ev_lc_prev)
                        ev_mm = kb.op("tensor", lambda e, h2=h2: e.matmul(
                            p_s6[0:1, 0:512], c_onesb[:, :], qq_sb[:, h2 % 2, :],
                            start=True, stop=True), waits=w_mm)
                        qn_guard[h2 % 2] = ev_mm
                        ev_r = kb.op("vector", lambda e, h2=h2: e.reduce_max(
                            norm_sb[0:1, 16 + h2:17 + h2], p_s6[0:1, 0:512], axis=AX.X),
                            waits=[ev_mm])
                        ev_qq_prev = ev_r
                        ev_q2.append(ev_r)

                    for h in range(NH):
                        i = wi[0]
                        wi[0] += 1
                        wbuf = wqo_sb.buf(i)
                        ev_wq = kb.dma("sync", wbuf[:, :],
                                       _ap2(wq, h * 128 * 2048, [2048, 128], [1, 2048]),
                                       waits=wqo_sb.guard(i) + [w_ready["q"]])
                        pqb = pv_ring[h % 2]
                        evm = kb.op("tensor", lambda e, wbuf=wbuf, pqb=pqb: tuple(
                            e.matmul(pqb[:, :], wbuf[:, hc * 128:(hc + 1) * 128],
                                     xn_q[:, hc, :], start=(hc == 0), stop=(hc == NCH - 1))
                            for hc in range(NCH)),
                            waits=[ev_wq, ev_xnq]
                            + ([pv_guard[h % 2]] if pv_guard[h % 2] else []))
                        eve = kb.op("vector", lambda e, h=h, pqb=pqb:
                                    e.tensor_copy(Q_all[:, h, :], pqb[:, :]), waits=[evm])
                        pv_guard[h % 2] = eve
                        wqo_sb.done(i, evm)
                        head_done.append(eve)
                        w_qq = [eve]
                        if qn_guard[h % 2]:
                            w_qq.append(qn_guard[h % 2])
                        ev_s = kb.op("vector", lambda e, h=h: e.tensor_tensor(
                            qq_sb[:, h % 2, :], Q_all[:, h, :], Q_all[:, h, :],
                            op=ALU.mult), waits=w_qq)
                        qn_pend.append((h, ev_s))
                        if len(qn_pend) > 1:
                            qn_flush(qn_pend.pop(0))
                    for it_ in qn_pend:
                        qn_flush(it_)
                    qn_pend.clear()
                    if Q == 0:
                        emit_v_loads()
                        emit_knorms()
                    ev_m2 = kb.op("vector", lambda e: e.tensor_tensor(
                        norm_sb[0:1, 32:48], norm_sb[0:1, 16:32], norm_sb[0:1, 0:16],
                        op=ALU.mult), waits=ev_q2 + ev_k2)
                    ev_m = kb.op("scalar", lambda e: e.activation(
                        norm_sb[0:1, 48:64], norm_sb[0:1, 32:48], AF.Sqrt),
                        waits=[ev_m2])
                    ev_mhm = kb.op("tensor", lambda e: e.matmul(
                        p_bc[:, 0:16], c_nsrow[0:1, :], norm_sb[0:1, 48:64],
                        start=True, stop=True),
                        waits=[ev_m] + ([ev_rbc_prev] if ev_rbc_prev else []))
                    ev_mhc = kb.op("vector", lambda e: e.tensor_copy(
                        mh_sb[:, :], p_bc[:, 0:16]), waits=[ev_mhm])
                    if DBG and s == 0 and Q == 0 and causal:
                        kb.dma("gpsimd", d_dbg_att.ap()[:, :], mh_sb[:, :], waits=[ev_mhc])
                        kb.dma("gpsimd", d_dbg_nrm.ap()[:, :], norm_sb[0:1, 0:80], waits=[ev_mhc])

                    for h in range(NH):
                        ev_qh = head_done[h]
                        n_ks = 4 * (Q + 1) if causal else 16
                        p_v = pv_ring[h % 2]
                        LA = 3
                        ev_a = None
                        ev_pv_last = None
                        st1 = {}
                        for kx in range(n_ks + LA):
                            if kx < n_ks:
                                ksx = kx
                                i = sti[0]
                                sti[0] += 1
                                pb = p_t.buf(i)
                                evm = kb.op("tensor", lambda e, h=h, ksx=ksx, pb=pb: e.matmul(
                                    pb[:, :], K_sb[:, h, ksx * 128:(ksx + 1) * 128],
                                    Q_all[:, h, :], start=True, stop=True),
                                    waits=[ev_qh, ev_Kq[h // 4]] + p_t.guard(i))
                                pti = pi[0]
                                pi[0] += 1
                                ptb = pt_sb.buf(pti)
                                eve = kb.op("scalar", lambda e, pb=pb, ptb=ptb, h=h: e.activation(
                                    ptb[:, :], pb[:, :], AF.Exp,
                                    bias=mh_sb[:, h:h + 1], scale=SCALE),
                                    waits=[evm, ev_mhc] + pt_sb.guard(pti))
                                p_t.done(i, eve)
                                ev_pt = eve
                                if causal and ksx >= 4 * Q:
                                    r = ksx - 4 * Q
                                    ev_pt = kb.op("vector", lambda e, r=r, ptb=ptb: e.tensor_tensor(
                                        ptb[:, :], ptb[:, :], c_mask[:, r, :], op=ALU.mult),
                                        waits=[eve] + EV_CONST)
                                st1[kx] = (pti, ptb, ev_pt)
                            if kx >= LA:
                                ksx = kx - LA
                                first = (ksx == 0)
                                last = (ksx == n_ks - 1)
                                pti, ptb, ev_pt = st1.pop(ksx)
                                w_a = [ev_pt]
                                if first and lm_guard[h % 2]:
                                    w_a.append(lm_guard[h % 2])
                                if first:
                                    ev_a = kb.op("vector", lambda e, h=h, ptb=ptb:
                                                 e.tensor_copy(accL[:, h % 2, :], ptb[:, :]),
                                                 waits=w_a)
                                else:
                                    ev_a = kb.op("vector", lambda e, h=h, ptb=ptb:
                                                 e.tensor_tensor(accL[:, h % 2, :],
                                                                 accL[:, h % 2, :],
                                                                 ptb[:, :], op=ALU.add),
                                                 waits=w_a)
                                w_pv = [ev_pt, ev_Vq[ksx // 4]]
                                if first and pv_guard[h % 2]:
                                    w_pv.append(pv_guard[h % 2])
                                ev_pv_last = kb.op("tensor",
                                                   lambda e, ksx=ksx, h=h, ptb=ptb, first=first, last=last, p_v=p_v:
                                                   e.matmul(p_v[:, :],
                                                            V_sb[:, ksx, h * 128:(h + 1) * 128],
                                                            ptb[:, :], start=first, stop=last),
                                                   waits=w_pv)
                                pt_sb.done(pti, ev_a, ev_pv_last)
                        fin_pend.append((h, ev_a, ev_pv_last, p_v))
                        if len(fin_pend) > 1:
                            fin_flush(fin_pend.pop(0))
                        precast_pump(3)
                    for it_ in fin_pend:
                        fin_flush(it_)
                    fin_pend.clear()
                    ev_o_last = None
                    for osub in range(NCH):
                        i = wi[0]
                        wi[0] += 1
                        wbuf = wqo_sb.buf(i)
                        ev_wo = kb.dma("sync", wbuf[:, :],
                                       _ap2(wo, osub * 128 * 2048, [2048, 128], [1, 2048]),
                                       waits=wqo_sb.guard(i) + [w_ready["o"]])
                        ri = r2i[0]
                        r2i[0] += 1
                        rbuf = res2.buf(ri)
                        ev_rl = kb.dma("sync", rbuf[:, :],
                                       _ap2(src_R, s * D * S + osub * 128 * S + q0,
                                            [S, 128], [1, 512]),
                                       waits=res2.guard(ri))
                        pb = pv_ring[osub % 2]
                        w_om = [ev_wo] + head_done
                        if pv_guard[osub % 2]:
                            w_om.append(pv_guard[osub % 2])
                        evm = kb.op("tensor", lambda e, wbuf=wbuf, pb=pb: tuple(
                            e.matmul(pb[:, :], wbuf[:, hc * 128:(hc + 1) * 128],
                                     attnT[:, hc, :], start=(hc == 0), stop=(hc == NCH - 1))
                            for hc in range(NCH)),
                            waits=w_om)
                        wqo_sb.done(i, evm)
                        ev_o_last = evm
                        ev_add = kb.op("vector", lambda e, pb=pb, rbuf=rbuf: e.tensor_tensor(
                            rbuf[:, :], rbuf[:, :], pb[:, :], op=ALU.add),
                            waits=[evm, ev_rl])
                        pv_guard[osub % 2] = ev_add
                        ev_st = kb.dma("gpsimd",
                                       _ap2(dst, s * D * S + osub * 128 * S + q0,
                                            [S, 128], [1, 512]),
                                       rbuf[:, :], waits=[ev_add])
                        res2.done(ri, ev_st)
                    attnT_free[0] = ev_o_last
            prev_join = kb.join()
        attn_ctx.close()
        return prev_join

    def emit_ffn(wg, wu, wd, src_R, dst, prev_join, w_ready):
        TB = 512
        for s in range(2):
            for tb in range(S // TB):
                tb0 = tb * TB
                with ExitStack() as phh:
                    hsT = phh.enter_context(nc.sbuf_tensor(uname("hsT"), [128, 64, TB], BF16))
                    with ExitStack() as ph:
                        xn_tb = ph.enter_context(
                            nc.sbuf_tensor(uname("xn_tb"), [128, NCH, TB], BF16))
                        with ExitStack() as p1s:
                            res_sb = Ring([p1s.enter_context(
                                nc.sbuf_tensor(uname("fres"), [128, NCH, 256], F32))
                                for _ in range(2)], entry=prev_join)
                            sqb = p1s.enter_context(
                                nc.sbuf_tensor(uname("fsq"), [128, 2, 256], F32))
                            rowb = p1s.enter_context(
                                nc.sbuf_tensor(uname("frow"), [1, 768], F32))
                            accF = p1s.enter_context(
                                nc.sbuf_tensor(uname("facc"), [128, 2, 256], F32))
                            pstat = p1s.enter_context(
                                nc.psum_tensor(uname("fstat"), [1, 512], F32))
                            pbc = p1s.enter_context(
                                nc.psum_tensor(uname("fbc"), [128, 512], F32))
                            entry = None
                            acc_g = None
                            stat_g = None
                            for tq in range(TB // 256):
                                t0 = tb0 + tq * 256
                                rbuf = res_sb.buf(tq)
                                ev_res = kb.dma(
                                    "sync",
                                    _ap3(rbuf, 0, [NCH * 256, 128], [256, NCH], [1, 256]),
                                    _ap3(src_R, s * D * S + t0, [S, 128], [128 * S, NCH], [1, 256]),
                                    waits=res_sb.guard(tq) + prev_join)
                                xn_evs, ev_stat, ev_statrd = emit_ln_full(
                                    256, lambda hc, rbuf=rbuf: rbuf[:, hc, :], [ev_res],
                                    lambda hc, tq=tq: xn_tb[:, hc, tq * 256:(tq + 1) * 256],
                                    pstat, pbc, sqb, rowb, accF, entry=entry,
                                    acc_guard=acc_g, stat_guard=stat_g)
                                acc_g = [ev_stat]
                                stat_g = [ev_statrd]
                                res_sb.done(tq, xn_evs[-1])
                                entry = [xn_evs[-1]]
                            ev_xn = kb.join()
                        with ExitStack() as p2s:
                            def asb2(name, shape, dtype):
                                return p2s.enter_context(
                                    nc.sbuf_tensor(uname(name), shape, dtype))
                            wgb = Ring([asb2("wgb", [128, NCH, 512], BF16) for _ in range(2)],
                                       entry=ev_xn)
                            wub = Ring([asb2("wub", [128, NCH, 512], BF16) for _ in range(2)],
                                       entry=ev_xn)
                            sg_sb = Ring([asb2("sg", [128, 512], F32) for _ in range(2)],
                                         entry=ev_xn)
                            h_sb = Ring([asb2("hsb", [128, 512], BF16) for _ in range(2)],
                                        entry=ev_xn)
                            a_sb = asb2("a_sb", [128, 512], BF16)
                            t_sb = asb2("t_sb", [128, 4, 128], BF16)
                            hs_sb = Ring([asb2("hss", [128, 512], BF16) for _ in range(2)],
                                         entry=ev_xn)
                            pg = Ring([p2s.enter_context(
                                nc.psum_tensor(uname("pg"), [128, 512], F32)) for _ in range(2)],
                                entry=ev_xn)
                            pu = Ring([p2s.enter_context(
                                nc.psum_tensor(uname("pu"), [128, 512], F32)) for _ in range(2)],
                                entry=ev_xn)
                            ptp = Ring([p2s.enter_context(
                                nc.psum_tensor(uname("ptp"), [128, 512], BF16)) for _ in range(2)],
                                entry=ev_xn)

                            ti = [0]
                            ev2_prev = None
                            pend = []

                            def flush_tr(item):
                                i, ic, ts, hsb, ev3 = item
                                ptb = ptp.buf(i)
                                evt = kb.op("tensor", lambda e, hsb=hsb, ptb=ptb: tuple(
                                    e.transpose(ptb[:, j * 128:(j + 1) * 128],
                                                hsb[:, j * 128:(j + 1) * 128], c_idb[:, :])
                                    for j in range(4)),
                                    waits=[ev3] + EV_CONST + ptp.guard(i))
                                hs_sb.done(i, evt)
                                eve = kb.op("vector", lambda e, ic=ic, ts=ts, ptb=ptb:
                                            e.tensor_copy(
                                                _ap3(hsT, (ic * 4) * TB + ts * 128,
                                                     [64 * TB, 128], [TB, 4], [1, 128]),
                                                _ap3(ptb, 0, [512, 128], [128, 4], [1, 128])),
                                            waits=[evt])
                                ptp.done(i, eve)

                            for ic in range(16):
                                precast_pump(6)
                                ev_w = []
                                for (ring, dram, wkey) in ((wgb, wg, "g"), (wub, wu, "u")):
                                    bufw = ring.buf(ic)
                                    evw = kb.dma(
                                        "sync",
                                        _ap3(bufw, 0, [NCH * 512, 128], [512, NCH], [1, 512]),
                                        _ap3(dram, ic * NCH * 128 * 512,
                                             [512, 128], [128 * 512, NCH], [1, 512]),
                                        waits=ring.guard(ic) + [w_ready[wkey]])
                                    ev_w.append(evw)
                                for ts in range(TB // 128):
                                    i = ti[0]
                                    ti[0] += 1
                                    pgb, pub = pg.buf(i), pu.buf(i)
                                    evg = kb.op("tensor", lambda e, ic=ic, ts=ts, pgb=pgb: tuple(
                                        e.matmul(pgb[:, :],
                                                 xn_tb[:, hc, ts * 128:(ts + 1) * 128],
                                                 wgb.buf(ic)[:, hc, :],
                                                 start=(hc == 0), stop=(hc == NCH - 1))
                                        for hc in range(NCH)),
                                        waits=ev_w + ev_xn + pg.guard(i))
                                    evu = kb.op("tensor", lambda e, ic=ic, ts=ts, pub=pub: tuple(
                                        e.matmul(pub[:, :],
                                                 xn_tb[:, hc, ts * 128:(ts + 1) * 128],
                                                 wub.buf(ic)[:, hc, :],
                                                 start=(hc == 0), stop=(hc == NCH - 1))
                                        for hc in range(NCH)),
                                        waits=ev_w + ev_xn + pu.guard(i))
                                    if ts == TB // 128 - 1:
                                        wgb.done(ic, evg)
                                        wub.done(ic, evu)
                                    sgb = sg_sb.buf(i)
                                    evs_ = kb.op("scalar", lambda e, pgb=pgb, sgb=sgb:
                                                 e.activation(sgb[:, :], pgb[:, :], AF.Silu),
                                                 waits=[evg] + sg_sb.guard(i))
                                    pg.done(i, evs_)
                                    hb = h_sb.buf(i)
                                    evh = kb.op("vector", lambda e, pub=pub, sgb=sgb, hb=hb:
                                                e.tensor_tensor(hb[:, :], sgb[:, :], pub[:, :],
                                                                op=ALU.mult),
                                                waits=[evs_, evu] + h_sb.guard(i))
                                    pu.done(i, evh)
                                    sg_sb.done(i, evh)
                                    w_abs = [evh]
                                    if ev2_prev:
                                        w_abs.append(ev2_prev)
                                    eva = kb.op("scalar", lambda e, hb=hb: e.activation(
                                        a_sb[:, :], hb[:, :], AF.Abs), waits=w_abs)
                                    ev1 = kb.op("vector", lambda e: (
                                        e.tensor_tensor(t_sb[:, 0, :], a_sb[:, 0:128],
                                                        a_sb[:, 128:256], op=ALU.max),
                                        e.tensor_tensor(t_sb[:, 1, :], a_sb[:, 0:128],
                                                        a_sb[:, 128:256], op=ALU.min),
                                        e.tensor_tensor(t_sb[:, 2, :], a_sb[:, 256:384],
                                                        a_sb[:, 384:512], op=ALU.max),
                                        e.tensor_tensor(t_sb[:, 3, :], a_sb[:, 256:384],
                                                        a_sb[:, 384:512], op=ALU.min),
                                        e.tensor_tensor(t_sb[:, 0, :], t_sb[:, 0, :],
                                                        t_sb[:, 2, :], op=ALU.min),
                                        e.tensor_tensor(t_sb[:, 1, :], t_sb[:, 1, :],
                                                        t_sb[:, 3, :], op=ALU.max),
                                        e.tensor_tensor(t_sb[:, 0, :], t_sb[:, 0, :],
                                                        t_sb[:, 1, :], op=ALU.max),
                                    ), waits=[eva])
                                    hsb = hs_sb.buf(i)
                                    ev2 = kb.op("vector", lambda e, hsb=hsb: tuple(
                                        e.tensor_tensor(hsb[:, j * 128:(j + 1) * 128],
                                                        a_sb[:, j * 128:(j + 1) * 128],
                                                        t_sb[:, 0, :], op=ALU.is_ge)
                                        for j in range(4)), waits=[ev1] + hs_sb.guard(i))
                                    ev2_prev = ev2
                                    ev3 = kb.op("vector", lambda e, hb=hb, hsb=hsb:
                                                e.tensor_tensor(hsb[:, :], hsb[:, :], hb[:, :],
                                                                op=ALU.mult),
                                                waits=[ev2])
                                    h_sb.done(i, ev3)
                                    pend.append((i, ic, ts, hsb, ev3))
                                    if len(pend) > 1:
                                        flush_tr(pend.pop(0))
                            for p_ in pend:
                                flush_tr(p_)
                            pend.clear()
                            ev_hsT = kb.join()
                    with ExitStack() as p3s:
                        def asb3(name, shape, dtype):
                            return p3s.enter_context(nc.sbuf_tensor(uname(name), shape, dtype))
                        wdb = Ring([asb3("wdb", [128, 64, 128], BF16) for _ in range(2)],
                                   entry=ev_hsT)
                        res3 = Ring([asb3("res3", [128, 512], F32) for _ in range(4)],
                                    entry=ev_hsT)
                        pd = Ring([p3s.enter_context(
                            nc.psum_tensor(uname("pd"), [128, 512], F32)) for _ in range(4)],
                            entry=ev_hsT)
                        for dsub in range(NCH):
                            bufw = wdb.buf(dsub)
                            evw = kb.dma("sync",
                                         _ap3(bufw, 0, [64 * 128, 128], [128, 64], [1, 128]),
                                         _ap3(wd, dsub * 128 * 64 * 128,
                                              [64 * 128, 128], [128, 64], [1, 128]),
                                         waits=wdb.guard(dsub) + [w_ready["d"]])
                            pdb = pd.buf(dsub)
                            ev_mm = kb.op(
                                "tensor", lambda e, bufw=bufw, pdb=pdb: tuple(
                                    e.matmul(pdb[:, :], bufw[:, isub, :],
                                             hsT[:, isub, :],
                                             start=(isub == 0), stop=(isub == 63))
                                    for isub in range(64)),
                                waits=[evw] + ev_hsT + pd.guard(dsub))
                            wdb.done(dsub, ev_mm)
                            rbuf = res3.buf(dsub)
                            ev_rl = kb.dma(
                                "sync", rbuf[:, :],
                                _ap2(src_R, s * D * S + dsub * 128 * S + tb0,
                                     [S, 128], [1, 512]),
                                waits=res3.guard(dsub))
                            ev_add = kb.op("vector", lambda e, pdb=pdb, rbuf=rbuf:
                                           e.tensor_tensor(rbuf[:, :], rbuf[:, :], pdb[:, :],
                                                           op=ALU.add),
                                           waits=[ev_mm, ev_rl])
                            pd.done(dsub, ev_add)
                            ev_st = kb.dma(
                                "gpsimd",
                                _ap2(dst, s * D * S + dsub * 128 * S + tb0,
                                     [S, 128], [1, 512]),
                                rbuf[:, :], waits=[ev_add])
                            res3.done(dsub, ev_st)
                prev_join = kb.join()
        return prev_join

    jn = list(EV_CONST)
    layers = [
        ("attn", "aq", "ak", "av", "ao", True),
        ("ffn",),
        ("attn", "bq", "bk", "bv", "bo", False),
        ("ffn",),
    ]
    src = d_x
    for li in range(N_SUB):
        dst = d_y if li == N_SUB - 1 else d_R
        if layers[li][0] == "attn":
            _, nq, nk, nv, no, causal = layers[li]
            for t in (nq, nk, nv, no):
                precast_drain(t)
            wr = {"q": ev_precast[nq], "k": ev_precast[nk],
                  "v": ev_precast[nv], "o": ev_precast[no]}
            jn = emit_attn(w_bf[nq], w_bf[nk], w_bf[nv], w_bf[no],
                           src, dst, causal, jn, wr)
        else:
            for t in ("g", "u", "d"):
                precast_drain(t)
            wr_f = {"g": ev_precast["g"], "u": ev_precast["u"], "d": ev_precast["d"]}
            jn = emit_ffn(w_bf["g"], w_bf["u"], w_bf["d"], src, dst, jn, wr_f)
        src = d_R

    final_evs = kb.join()
    kb.op("gpsimd", lambda e: e.sem_inc(kb.sem["gpsimd"], 1),
          waits=final_evs, inc=False)
    kb.op("sync", lambda e: e.sem_inc(kb.sem["sync"], 1),
          waits=final_evs, inc=False)

    with nc.allow_low_precision("bf16 matmul operands / bf16 activation stores"):
        kb.build()
    alloc.close()
    return nc


def _lhsT_tile(W):
    return np.ascontiguousarray(
        W.reshape(16, 128, 16, 128).transpose(0, 3, 2, 1).reshape(16, 128, 2048))


def _rhsT_tile(W):
    return np.ascontiguousarray(W.T.reshape(16, 128, 2048))


def _ffn_gu_tile(W):
    Wp = W.reshape(2048, 16, 128, 4).transpose(0, 1, 3, 2).reshape(2048, 8192)
    return np.ascontiguousarray(Wp.reshape(16, 128, 16, 512).transpose(2, 0, 1, 3))


def _ffn_d_tile(W):
    Wp = W.reshape(16, 128, 4, 2048).transpose(0, 2, 1, 3).reshape(8192, 2048)
    return np.ascontiguousarray(Wp.reshape(64, 128, 16, 128).transpose(2, 1, 0, 3))


def kernel(L0_attn_a_hidden_states, L0_attn_a_w_qkv, L0_attn_a_w_o,
           L0_attn_b_Wq, L0_attn_b_Wk, L0_attn_b_Wv, L0_attn_b_Wo,
           L0_ffn_w_gate, L0_ffn_w_up, L0_ffn_w_down):
    x = np.asarray(L0_attn_a_hidden_states, dtype=np.float32)
    wqkv = np.asarray(L0_attn_a_w_qkv, dtype=np.float32)
    woa = np.asarray(L0_attn_a_w_o, dtype=np.float32)

    weights = {
        "w_aq": _lhsT_tile(wqkv[0:2048]),
        "w_ak": _lhsT_tile(wqkv[2048:4096]),
        "w_av": _rhsT_tile(wqkv[4096:6144]),
        "w_ao": _lhsT_tile(woa),
        "w_bq": _lhsT_tile(np.asarray(L0_attn_b_Wq, dtype=np.float32)),
        "w_bk": _lhsT_tile(np.asarray(L0_attn_b_Wk, dtype=np.float32)),
        "w_bv": _rhsT_tile(np.asarray(L0_attn_b_Wv, dtype=np.float32)),
        "w_bo": _lhsT_tile(np.asarray(L0_attn_b_Wo, dtype=np.float32)),
        "w_g": _ffn_gu_tile(np.asarray(L0_ffn_w_gate, dtype=np.float32)),
        "w_u": _ffn_gu_tile(np.asarray(L0_ffn_w_up, dtype=np.float32)),
        "w_d": _ffn_d_tile(np.asarray(L0_ffn_w_down, dtype=np.float32)),
    }
    consts = {
        "ident_bf": np.eye(128, dtype=ml_dtypes.bfloat16),
        "ident_f32": np.eye(128, dtype=np.float32),
        "ones_col": np.ones((128, 1), dtype=np.float32),
        "ones_bf": np.ones((128, 1), dtype=ml_dtypes.bfloat16),
        "scale_row": np.full((1, 128), SCALE, dtype=np.float32),
        "nscale_row": np.full((1, 128), -SCALE, dtype=np.float32),
        "ones_row": np.ones((1, 128), dtype=np.float32),
    }
    r_ = np.arange(4)[:, None, None]
    p_ = np.arange(128)[None, :, None]
    f_ = np.arange(512)[None, None, :]
    consts["mask01"] = (f_ >= 128 * r_ + p_).astype(ml_dtypes.bfloat16)

    nc = build_program()

    in_maps = []
    for c in range(N_CORES):
        m = {"x": np.ascontiguousarray(x[2 * c:2 * c + 2].transpose(0, 2, 1))}
        m.update(weights)
        m.update(consts)
        in_maps.append(m)

    res = run_bass_kernel_spmd(nc, in_maps, core_ids=list(range(N_CORES)))
    out = np.concatenate([r["y"].transpose(0, 2, 1) for r in res.results], axis=0)
    return np.ascontiguousarray(out)
